# version 11
# baseline (speedup 1.0000x reference)
import contextlib

import numpy as np
import ml_dtypes

import concourse.bass as bass
import concourse.mybir as mybir
from concourse.bass_utils import run_bass_kernel_spmd

F32 = mybir.dt.float32
BF16 = mybir.dt.bfloat16
AF = mybir.ActivationFunctionType
ALU = mybir.AluOpType
AXL = mybir.AxisListType

T = 4096
D = 1024
C = 8
DFF = 4096
CF = 32
DI = 2048
CI = 16
NPAIR = 8
NLAYER = 6
NE = 7
NT = 512
NTILES = 8
NPASS = 4
EPS_LN = 1e-5
EPS_ATTN = 1e-6
BIG = 1.0e9

ENGS = ("tensor", "vector", "scalar", "sync", "gpsimd")
Q2E = {"sy_q": "sync", "sc_q": "scalar", "gp_q": "gpsimd"}


class MK:
    def __init__(self, nc, ctx, debug_stop=None):
        self.nc = nc
        self.ctx = ctx
        self.debug_stop = debug_stop
        self.ops = {e: [] for e in ENGS}
        self.sems = {}
        self.counts = {}
        for name in ("pe", "ve", "ae"):
            self.sems[name] = ctx.enter_context(nc.semaphore(name))
            self.counts[name] = 0
        self.last_wait = {e: {} for e in ENGS}
        self.dma_last = {}
        self.declare_tensors()
        self.bank_free = [None] * 8
        self.buf_tok = {}

    def tok(self, sem):
        return (sem, self.counts[sem])

    def wait(self, eng, token):
        if token is None:
            return
        if isinstance(token, (list, tuple)) and token and \
                not isinstance(token[0], str):
            for t in token:
                self.wait(eng, t)
            return
        sem, val = token
        if val <= 0 or self.last_wait[eng].get(sem, 0) >= val:
            return
        self.last_wait[eng][sem] = val
        h = self.sems[sem]
        self.ops[eng].append(lambda e, h=h, v=val: e.wait_ge(h, v))

    def emit(self, eng, fn, inc=None, inc_amt=1, deps=()):
        for t in deps:
            self.wait(eng, t)
        token = None
        if inc is not None:
            self.counts[inc] += inc_amt
            token = (inc, self.counts[inc])
            h = self.sems[inc]
            self.ops[eng].append(
                lambda e, fn=fn, h=h, amt=inc_amt: fn(e).then_inc(h, amt))
        else:
            self.ops[eng].append(fn)
        return token

    def pe(self, fn, inc=False, deps=()):
        return self.emit("tensor", fn, "pe" if inc else None, deps=deps)

    def ve(self, fn, inc=False, deps=()):
        return self.emit("vector", fn, "ve" if inc else None, deps=deps)

    def ae(self, fn, inc=False, deps=()):
        return self.emit("scalar", fn, "ae" if inc else None, deps=deps)

    def dma(self, queue, key, fn, deps=()):
        eng = Q2E[queue]
        kname = "d_" + key
        if kname not in self.sems:
            self.sems[kname] = self.ctx.enter_context(self.nc.semaphore(kname))
            self.counts[kname] = 0
        prev = self.dma_last.get(kname)
        dd = tuple(deps)
        if prev is not None:
            dd = dd + (prev,)
        tok = self.emit(eng, fn, kname, inc_amt=16, deps=dd)
        self.dma_last[kname] = tok
        return tok

    def bank_wait(self, i):
        return self.bank_free[i]

    def bank_consumed(self, i, token):
        self.bank_free[i] = token

    def declare_tensors(self):
        nc, E_ = self.nc, self.ctx.enter_context
        dt = nc.dram_tensor
        self.t_x = dt("L0_attn_x", [T, D], F32, kind="ExternalInput")
        self.t_Wq = dt("L0_attn_Wq", [NLAYER, D, D], F32, kind="ExternalInput")
        self.t_bq = dt("L0_attn_bq", [NLAYER, D], F32, kind="ExternalInput")
        self.t_Wk = dt("L0_attn_Wk", [NLAYER, D, D], F32, kind="ExternalInput")
        self.t_bk = dt("L0_attn_bk", [NLAYER, D], F32, kind="ExternalInput")
        self.t_Wv = dt("L0_attn_Wv", [NLAYER, D, D], F32, kind="ExternalInput")
        self.t_bv = dt("L0_attn_bv", [NLAYER, D], F32, kind="ExternalInput")
        self.t_Wo = dt("L0_attn_Wo", [NLAYER, D, D], F32, kind="ExternalInput")
        self.t_bo = dt("L0_attn_bo", [NLAYER, D], F32, kind="ExternalInput")
        self.t_ln1w = dt("L0_attn_ln1_w", [NLAYER, D], F32, kind="ExternalInput")
        self.t_ln1b = dt("L0_attn_ln1_b", [NLAYER, D], F32, kind="ExternalInput")
        self.t_W1 = dt("L0_attn_W1", [NLAYER, D, DFF], F32, kind="ExternalInput")
        self.t_b1 = dt("L0_attn_b1", [NLAYER, DFF], F32, kind="ExternalInput")
        self.t_W2 = dt("L0_attn_W2", [NLAYER, DFF, D], F32, kind="ExternalInput")
        self.t_b2 = dt("L0_attn_b2", [NLAYER, D], F32, kind="ExternalInput")
        self.t_ln2w = dt("L0_attn_ln2_w", [NLAYER, D], F32, kind="ExternalInput")
        self.t_ln2b = dt("L0_attn_ln2_b", [NLAYER, D], F32, kind="ExternalInput")
        self.t_lnfw = dt("L0_attn_lnf_w", [D], F32, kind="ExternalInput")
        self.t_lnfb = dt("L0_attn_lnf_b", [D], F32, kind="ExternalInput")
        self.t_sg = dt("L0_ffn_shared_gate_w", [D, DI], F32, kind="ExternalInput")
        self.t_su = dt("L0_ffn_shared_up_w", [D, DI], F32, kind="ExternalInput")
        self.t_sd = dt("L0_ffn_shared_down_w", [DI, D], F32, kind="ExternalInput")
        self.t_rg = dt("L0_ffn_routed_gate_w", [NE, D, DI], F32, kind="ExternalInput")
        self.t_ru = dt("L0_ffn_routed_up_w", [NE, D, DI], F32, kind="ExternalInput")
        self.t_rd = dt("L0_ffn_routed_down_w", [NE, DI, D], F32, kind="ExternalInput")
        self.t_rw = dt("L0_ffn_router_w", [D, NE], F32, kind="ExternalInput")
        self.t_rb = dt("L0_ffn_routing_bias", [NE], F32, kind="ExternalInput")
        self.t_id = dt("ident", [128, 128], F32, kind="ExternalInput")
        self.t_out = dt("out", [T, D], F32, kind="ExternalOutput")
        self.t_S0 = dt("S0", [D, T], F32)
        self.t_S1 = dt("S1", [D, T], F32)
        self.t_X1 = dt("X1", [D, T], F32)
        self.t_US = dt("US", [DFF, T], BF16)
        self.t_LT = dt("LTd", [35, 14336], BF16)
        self.t_WB = dt("WB", [120, 128, 8192], BF16)

        st, pt = nc.sbuf_tensor, nc.psum_tensor
        self.pb = [E_(pt(f"pb{i}", [128, 512], F32)) for i in range(8)]
        self.ident = E_(st("identt", [128, 128], F32))
        self.ones_lhsT = E_(st("ones_lhsT", [1, 128], BF16))
        self.ones_lhsT_f = E_(st("ones_lhsT_f", [1, 128], F32))
        self.ones_col = E_(st("ones_col", [128, 1], BF16))
        self.E2 = E_(st("E2", [2, 128], BF16))
        self.lnLT = E_(st("lnLT", [35, 2048], BF16))
        self.lnLTs = E_(st("lnLTs", [35, 1152], BF16))
        self.bq_col = E_(st("bq_col", [128, NLAYER * C], F32))
        self.bo_col = E_(st("bo_col", [128, NLAYER * C], F32))
        self.b2_col = E_(st("b2_col", [128, NLAYER * C], F32))
        self.b1_col = E_(st("b1_col", [128, NLAYER * CF], F32))
        self.lnb_col = E_(st("lnb_col", [128, 13 * C], F32))
        self.rw_lhsT = E_(st("rw_lhsT", [128, C * NE], F32))
        self.colw_row = E_(st("colw_row", [1, NE], F32))
        self.rbias_row = E_(st("rbias_row", [1, NE], F32))
        self.rbias_tok = E_(st("rbias_tok", [128, NE], F32))
        self.RS = [E_(st(f"RS{i}", [35, T], BF16)) for i in range(2)]
        self.r_a = E_(st("r_a", [1, 1024], F32))
        self.r_b = E_(st("r_b", [1, 1024], F32))
        self.r_t = E_(st("r_t", [1, NT], F32))
        self.rowstage = E_(st("rowstage", [1, 3072], BF16))
        self.BD = E_(st("BD", [128, NPAIR * 128], BF16))
        self.zsb = E_(st("zsb", [128, NPAIR * 2], BF16))
        self.zrec = E_(st("zrec", [2, NT], BF16))
        self.zrec2 = E_(st("zrec2", [2, NT], BF16))
        self.bk_row = E_(st("bk_row", [1, D], BF16))
        self.bv_row = E_(st("bv_row", [1, D], BF16))
        self.zbs = E_(st("zbs", [128, NT], F32))
        self.wbs = E_(st("wbs", [128, NT], F32))
        self.lg_sb = E_(st("lg_sb", [7, NT], F32))
        self.lgt = E_(st("lgt", [128, 4 * NE], F32))
        self.pr_t = E_(st("pr_t", [128, 4 * NE], F32))
        self.msk = E_(st("msk", [128, 4 * NE], F32))
        self.m_col = E_(st("m_col", [128, 12], F32))
        self.w_tok = E_(st("w_tok", [128, 4 * NE], F32))
        self.a_colT = E_(st("a_colT", [128, 8], F32))
        self.wrow = E_(st("wrow", [7, 1024], BF16))
        self.wrowF = E_(st("wrowF", [1, 7 * NT], BF16))
        self.xbf = [E_(st(f"xbf{i}", [128, NT], BF16)) for i in range(2)]
        self.xsq = E_(st("xsq", [128, NT], BF16))
        self.tA = E_(st("tA", [128, NT], F32))
        self.tB = E_(st("tB", [128, NT], F32))
        self.tC = E_(st("tC", [128, NT], F32))
        self.arF = E_(st("arF", [128, 9 * 1024], F32))
        self.arB = E_(st("arB", [128, 48 * 1024], BF16))

    def aF(self, off, n):
        return self.arF[:, off:off + n]

    def aB(self, off, n):
        return self.arB[:, off:off + n]

    def transpose_into(self, bank, in_ap, n, dep=()):
        k = in_ap.shape[0]
        pbt = self.pb[bank]
        dd = (self.bank_wait(bank),) + tuple(dep)
        return self.pe(
            lambda e, b=pbt, i=in_ap, idn=self.ident[0:k, 0:k], n=n, k=k:
                e.transpose(b[0:n, 0:k], i, idn),
            inc=True, deps=dd)

    def stats_chunk(self, src_f32, deps, first, last, sum_bank, sq_bank, key):
        P = self
        n = src_f32.shape[-1]
        i = self.buf_tok.get((key, "xbfi"), 0)
        xb = self.xbf[i][:, 0:n]
        cv = P.ve(lambda e: e.tensor_copy(xb, src_f32), inc=True,
                  deps=tuple(deps) + self.buf_tok.get((key, "xbf", i), ()))
        cq = P.ae(lambda e: e.activation(self.xsq[:, 0:n], xb, AF.Square,
                                         bias=np.float32(0.0), scale=np.float32(1.0)),
                  inc=True, deps=(cv, self.buf_tok.get((key, "xsq"))))
        mm_deps = [cv]
        if first:
            mm_deps += [self.bank_wait(sum_bank), self.bank_wait(sq_bank)]
        t1 = P.pe(lambda e: e.matmul(self.pb[sum_bank][0:1, 0:n],
                                     self.ones_col[:, :], xb,
                                     start=first, stop=last),
                  inc=True, deps=tuple(mm_deps))
        t2 = P.pe(lambda e: e.matmul(self.pb[sq_bank][0:1, 0:n],
                                     self.ones_col[:, :], self.xsq[:, 0:n],
                                     start=first, stop=last),
                  inc=True, deps=(cq,))
        self.buf_tok[(key, "xbf", i)] = (t1, cq)
        self.buf_tok[(key, "xsq")] = t2
        self.buf_tok[(key, "xbfi")] = 1 - i
        return t1, t2

    def ln_rows(self, sum_bank, sq_bank, pe_toks, n, rs, rs_col, keep_ab_at=None):
        P = self
        pbS, pbQ = self.pb[sum_bank], self.pb[sq_bank]
        if keep_ab_at is not None:
            ra = self.r_a[0:1, keep_ab_at:keep_ab_at + n]
            rb = self.r_b[0:1, keep_ab_at:keep_ab_at + n]
        else:
            ra = self.r_a[0:1, 0:n]
            rb = self.r_b[0:1, 0:n]
        rt = self.r_t[0:1, 0:n]
        t1 = P.ae(lambda e: e.activation(rt, pbS[0:1, 0:n], AF.Square,
                                         bias=np.float32(0.0), scale=np.float32(1.0 / D)),
                  inc=True,
                  deps=tuple(pe_toks) + (self.buf_tok.get("ln_rt"),
                                         self.buf_tok.get("ra_free"),))
        t2 = P.ve(lambda e: e.scalar_tensor_tensor(
            rt, pbQ[0:1, 0:n], 1.0 / D, rt, ALU.mult, ALU.subtract),
            inc=True, deps=tuple(pe_toks) + (t1,))
        t3a = P.ae(lambda e: e.activation(rt, rt, AF.Sqrt,
                                          bias=np.float32(EPS_LN), scale=np.float32(1.0)),
                   inc=True, deps=(t2,))
        t3 = P.ve(lambda e: e.reciprocal(ra, rt), inc=True, deps=(t3a,))
        t4 = P.ae(lambda e: e.activation(rt, pbS[0:1, 0:n], AF.Identity,
                                         bias=np.float32(0.0), scale=np.float32(-1.0 / D)),
                  inc=True, deps=(t2, t3))
        self.bank_consumed(sum_bank, t4)
        self.bank_consumed(sq_bank, t2)
        t5 = P.ve(lambda e: e.tensor_mul(rb, rt, ra), inc=True, deps=(t3, t4))
        rs_ = self.rowstage
        rsdep = tuple(self.buf_tok.get("rowstage", ()))
        tb = self.tB[0:1, 0:n]
        tbdep = (self.buf_tok.get("ln_tb"),)
        c1 = P.ve(lambda e: e.tensor_copy(rs_[0:1, 0:n], ra), inc=True,
                  deps=(t3,) + rsdep)
        c2 = P.ve(lambda e: e.tensor_copy(rs_[0:1, 2 * n:3 * n], ra), inc=True,
                  deps=(t3,))
        c3 = P.ve(lambda e: e.tensor_copy(tb, rs_[0:1, 0:n]), inc=True,
                  deps=(c1,) + tbdep)
        c4 = P.ve(lambda e: e.tensor_sub(rs_[0:1, n:2 * n], ra, tb), inc=True,
                  deps=(c3,))
        c5 = P.ve(lambda e: e.tensor_copy(rs_[0:1, 3 * n:4 * n], rb), inc=True,
                  deps=(t5,))
        c6 = P.ve(lambda e: e.tensor_copy(rs_[0:1, 5 * n:6 * n], rb), inc=True,
                  deps=(t5,))
        c7 = P.ve(lambda e: e.tensor_copy(tb, rs_[0:1, 3 * n:4 * n]), inc=True,
                  deps=(c5, c4))
        c8 = P.ve(lambda e: e.tensor_sub(rs_[0:1, 4 * n:5 * n], rb, tb),
                  inc=True, deps=(c7,))
        self.buf_tok["ln_tb"] = c8
        d1 = P.dma("sc_q", "rs_a", lambda e: e.dma_start(
            bass.AP(rs, rs_col, [[T, 3], [1, n]]),
            rs_[0:1, 0:3 * n]), deps=(c2, c4))
        d2 = P.dma("sc_q", "rs_b", lambda e: e.dma_start(
            bass.AP(rs, 32 * T + rs_col, [[T, 3], [1, n]]),
            rs_[0:1, 3 * n:6 * n]), deps=(c6, c8))
        self.buf_tok["rowstage"] = (d1, d2)
        self.buf_tok["ln_rt"] = t5
        return (d1, d2), t3, t5

    def apply_ln(self, lhsT_t, lhsT_c0, lhsT_w, rs, rs_col, n, src_ap,
                 dst_f32, dst_bf, bias_col, deps, rs_dep=(), banks=(6, 7)):
        P = self
        bA, bB = banks
        rsA = rs[0:3, rs_col:rs_col + n]
        rsB = rs[32:35, rs_col:rs_col + n]
        lhA = lhsT_t[0:3, lhsT_c0:lhsT_c0 + lhsT_w]
        lhB = lhsT_t[32:35, lhsT_c0:lhsT_c0 + lhsT_w]
        tA_ = P.pe(lambda e: e.matmul(self.pb[bA][:, 0:n], lhA, rsA,
                                      start=True, stop=True),
                   inc=True,
                   deps=(self.bank_wait(bA),) + tuple(rs_dep))
        tB_ = P.pe(lambda e: e.matmul(self.pb[bB][:, 0:n], lhB, rsB,
                                      start=True, stop=True),
                   inc=True, deps=(self.bank_wait(bB),))
        tc = self.tC[:, 0:n]
        v1 = P.ve(lambda e: e.tensor_mul(tc, src_ap, self.pb[bA][:, 0:n]),
                  inc=True, deps=(tA_, self.buf_tok.get("apply_tC")) +
                  tuple(deps))
        self.bank_consumed(bA, v1)
        dst1 = dst_f32 if dst_f32 is not None else dst_bf
        v2 = P.ve(lambda e: e.scalar_tensor_tensor(
            dst1, tc, bias_col, self.pb[bB][:, 0:n], ALU.add, ALU.add),
            inc=True, deps=(tB_, v1))
        self.bank_consumed(bB, v2)
        self.buf_tok["apply_tC"] = v2
        v3 = None
        if dst_f32 is not None and dst_bf is not None:
            v3 = P.ve(lambda e: e.tensor_copy(dst_bf, dst_f32), inc=True,
                      deps=(v2,))
        return v2, (v3 if v3 is not None else v2)

    def emit_precast(self):
        P = self
        self.wb_tok = {}
        units = []
        ffn_mats = [("gu", self.t_sg, 0), ("gu", self.t_su, 0),
                    ("d", self.t_sd, 0)]
        for e in range(NE):
            ffn_mats += [("gu", self.t_rg, e * D * DI),
                         ("gu", self.t_ru, e * D * DI),
                         ("d", self.t_rd, e * DI * D)]
        for mi, (kind, tt, eoff) in enumerate(ffn_mats):
            for h in range(2):
                if kind == "gu":
                    src = bass.AP(tt, eoff + h * 1024,
                                  [[DI, 128], [128 * DI, C], [1, 1024]])
                else:
                    src = bass.AP(tt, eoff + h * 1024 * D,
                                  [[D, 128], [128 * D, C], [1, 1024]])
                units.append((mi * 2 + h, src))
        for li in range(NLAYER):
            u0 = 48 + li * 12
            for j, tt in enumerate((self.t_Wk, self.t_Wv, self.t_Wq, self.t_Wo)):
                units.append((u0 + j, bass.AP(
                    tt, li * D * D, [[D, 128], [128 * D, C], [1, 1024]])))
            for qi in range(4):
                units.append((u0 + 4 + qi, bass.AP(
                    self.t_W1, li * D * DFF + qi * 1024,
                    [[DFF, 128], [128 * DFF, C], [1, 1024]])))
            for qi in range(4):
                units.append((u0 + 8 + qi, bass.AP(
                    self.t_W2, li * DFF * D + qi * 1024 * D,
                    [[D, 128], [128 * D, C], [1, 1024]])))
        for u, src in units:
            dst = bass.AP(self.t_WB, u * 128 * 8192,
                          [[8192, 128], [1024, 8], [1, 1024]])
            self.wb_tok[u] = P.dma("gp_q", "pc", lambda e, dst=dst, src=src:
                                   e.dma_start(dst, src))

    def wb_load(self, key, unit, arb_off, deps=()):
        src = bass.AP(self.t_WB, unit * 128 * 8192, [[8192, 128], [1, 8192]])
        return self.dma("sc_q", key, lambda e, src=src, arb_off=arb_off:
                        e.dma_start(self.aB(arb_off, 8192), src),
                        deps=tuple(deps) + (self.wb_tok[unit],))

    def emit_setup(self):
        P = self
        d_id = P.dma("sy_q", "id", lambda e: e.dma_start(self.ident[:, :],
                                                   self.t_id.ap()))
        P.ve(lambda e: e.memset(self.ones_lhsT[:, :], 1.0))
        P.ve(lambda e: e.memset(self.ones_lhsT_f[:, :], 1.0))
        P.ve(lambda e: e.memset(self.ones_col[:, :], 1.0))
        P.ve(lambda e: e.memset(self.rowstage[0:1, 0:64], 1.0))
        P.ve(lambda e: e.memset(self.rowstage[0:1, 64:192], 0.0))
        m1 = P.ve(lambda e: e.memset(self.rowstage[0:1, 192:256], 1.0), inc=True)
        dE2 = P.dma("sy_q", "st", lambda e: e.dma_start(
            bass.AP(self.E2, 0, [[128, 2], [1, 128]]),
            self.rowstage[0:1, 0:256]), deps=(m1,))
        self.buf_tok["rowstage"] = (dE2,)

        stage = self.arF
        jobs = [
            (self.t_bq, D, NLAYER, self.bq_col, C, 0),
            (self.t_bo, D, NLAYER, self.bo_col, C, 0),
            (self.t_b2, D, NLAYER, self.b2_col, C, 0),
            (self.t_b1, DFF, NLAYER, self.b1_col, CF, 0),
            (self.t_ln1b, D, NLAYER, self.lnb_col, C, 0),
            (self.t_ln2b, D, NLAYER, self.lnb_col, C, 6 * C),
            (self.t_lnfb, D, 1, self.lnb_col, C, 12 * C),
        ]
        prev = d_id
        for (tt, width, nl, col_t, nch, base) in jobs:
            dld = P.dma("sy_q", "st", lambda e, tt=tt, nl=nl, width=width: e.dma_start(
                stage[0:nl, 0:width],
                bass.AP(tt, 0, [[width, nl], [1, width]])), deps=(prev,))
            for ch in range(nch):
                bk = 4 + (ch % 2)
                tp = self.transpose_into(
                    bk, stage[0:nl, ch * 128:(ch + 1) * 128], 128, dep=(dld,))
                ev = P.ve(lambda e, b=bk, nl=nl, col_t=col_t, base=base,
                          ch=ch, nch=nch: e.tensor_copy(
                    bass.AP(col_t, base + ch,
                            [[col_t.shape[1], 128], [nch, nl]]),
                    self.pb[b][0:128, 0:nl]), inc=True, deps=(tp,))
                self.bank_consumed(bk, ev)
            prev = ev

        drw = P.dma("sy_q", "st", lambda e: e.dma_start(
            self.rw_lhsT[:, :],
            bass.AP(self.t_rw, 0, [[NE, 128], [128 * NE, C], [1, NE]])))
        drb = P.dma("sy_q", "st", lambda e: e.dma_start(
            self.rbias_row[0:1, :], bass.AP(self.t_rb, 0, [[NE, 1], [1, NE]])))
        onef = self.arF[:, 8192:8193]
        mo = P.ve(lambda e: e.memset(onef, 1.0), inc=True, deps=(prev,))
        tl = None
        for ch in range(C):
            dd = (drw, mo, self.bank_wait(3)) if ch == 0 else ()
            tl = P.pe(lambda e, ch=ch: e.matmul(
                self.pb[3][0:1, 0:NE], onef,
                self.rw_lhsT[:, ch * NE:(ch + 1) * NE],
                start=(ch == 0), stop=(ch == C - 1)),
                inc=(ch == C - 1), deps=dd)
        e1 = P.ve(lambda e: e.tensor_copy(self.colw_row[0:1, :],
                                          self.pb[3][0:1, 0:NE]),
                  inc=True, deps=(tl,))
        self.bank_consumed(3, e1)
        t_rt = P.pe(lambda e: e.matmul(
            self.pb[3][:, 0:NE], self.ones_lhsT_f[0:1, :], self.rbias_row[0:1, :],
            start=True, stop=True), inc=True,
            deps=(drb, self.bank_wait(3), self.tok("ve")))
        e2 = P.ve(lambda e: e.tensor_copy(self.rbias_tok[:, :],
                                          self.pb[3][:, 0:NE]),
                  inc=True, deps=(t_rt,))
        self.bank_consumed(3, e2)

        wrow_f = self.arF[0:1, 4096:4096 + 1024]
        whi_f = self.arF[0:1, 4096 + 1024:4096 + 2048]
        prev_dma = drb
        for idx in range(13):
            if idx < 6:
                src, off = self.t_ln1w, idx * D
            elif idx < 12:
                src, off = self.t_ln2w, (idx - 6) * D
            else:
                src, off = self.t_lnfw, 0
            dl = P.dma("sy_q", "st", lambda e, src=src, off=off: e.dma_start(
                wrow_f, bass.AP(src, off, [[D, 1], [1, D]])),
                deps=(prev_dma, self.buf_tok.get("wrowf_free")))
            h1 = P.ve(lambda e: e.tensor_copy(self.rowstage[0:1, 0:1024],
                                              wrow_f),
                      inc=True,
                      deps=(dl,) + tuple(self.buf_tok.get("rowstage", ())))
            h2 = P.ve(lambda e: e.tensor_copy(self.rowstage[0:1, 1024:2048],
                                              wrow_f), inc=True, deps=(dl,))
            h3 = P.ve(lambda e: e.tensor_copy(whi_f, self.rowstage[0:1, 0:1024]),
                      inc=True, deps=(h1,))
            h4 = P.ve(lambda e: e.tensor_sub(self.rowstage[0:1, 2048:3072],
                                             wrow_f, whi_f), inc=True,
                      deps=(h3,))
            self.buf_tok["wrowf_free"] = h4
            ds = P.dma("sc_q", "lt", lambda e, idx=idx: e.dma_start(
                bass.AP(self.t_LT, idx * 1024, [[14336, 3], [1, 1024]]),
                self.rowstage[0:1, 0:3072]), deps=(h2, h4))
            ds2 = P.dma("sc_q", "lt", lambda e, idx=idx: e.dma_start(
                bass.AP(self.t_LT, 32 * 14336 + idx * 1024,
                        [[14336, 3], [1, 1024]]),
                self.rowstage[0:1, 0:3072]))
            self.buf_tok["rowstage"] = (ds, ds2)
            prev_dma = dl
        mz1 = P.ve(lambda e: e.memset(self.rowstage[0:1, 0:256], 1.0),
                   inc=True, deps=tuple(self.buf_tok["rowstage"]))
        mz2 = P.ve(lambda e: e.memset(self.rowstage[0:1, 256:384], 0.0),
                   inc=True)
        dmac = P.dma("sc_q", "lt", lambda e: e.dma_start(
            bass.AP(self.t_LT, 13 * 1024, [[14336, 3], [1, 128]]),
            self.rowstage[0:1, 0:384]), deps=(mz1, mz2))
        dmac2 = P.dma("sc_q", "lt", lambda e: e.dma_start(
            bass.AP(self.t_LT, 32 * 14336 + 13 * 1024, [[14336, 3], [1, 128]]),
            self.rowstage[0:1, 0:384]))
        self.buf_tok["rowstage"] = (dmac, dmac2)
        dls = []
        for pb_, do in ((0, 0), (32, 32 * 14336)):
            dls.append(P.dma("sy_q", "st", lambda e, pb_=pb_, do=do: e.dma_start(
                self.lnLTs[pb_:pb_ + 3, 0:1024],
                bass.AP(self.t_LT, do + 12 * 1024, [[14336, 3], [1, 1024]])),
                deps=(prev_dma, dmac2)))
            dls.append(P.dma("sy_q", "st", lambda e, pb_=pb_, do=do: e.dma_start(
                self.lnLTs[pb_:pb_ + 3, 1024:1152],
                bass.AP(self.t_LT, do + 13 * 1024, [[14336, 3], [1, 128]])),
                deps=(dmac2,)))
        self.buf_tok["setup"] = (dE2, e1, e2, self.tok("ve")) + tuple(dls)

    def emit_ffn(self, call, src_stream=None):
        P = self
        A_ACC, A_STG = 0, 8192
        B_XN, B_U, B_W = 0, 8192, 24576
        wslot_load = [None, None, None]
        wslot_free = [None, None, None]

        def acc(c, g=None):
            if g is None:
                return self.aF(A_ACC + c * 1024, 1024)
            return self.aF(A_ACC + c * 1024 + g * NT, NT)

        def xn(c, g):
            return self.aB(B_XN + c * 1024 + g * NT, NT)

        def uu(ci, g):
            return self.aB(B_U + ci * 1024 + g * NT, NT)

        af = self.buf_tok.get("arB_free")
        if af is not None:
            wslot_free[:] = [af, af, af]
            for ci in range(CI):
                for g_ in range(2):
                    self.buf_tok[("ufree", ci, g_)] = af
            self.buf_tok["apply_arB"] = af
        mats = [("g", self.t_sg, 0), ("u", self.t_su, 0), ("d", self.t_sd, 0)]
        for e in range(NE):
            mats += [("g", self.t_rg, e * D * DI), ("u", self.t_ru, e * D * DI),
                     ("d", self.t_rd, e * DI * D)]

        slot_i = [0]

        def load_half(mi, half):
            s = slot_i[0]
            slot_i[0] = (s + 1) % 3
            tok = self.wb_load(f"ws{s}", mi * 2 + half, B_W + s * 8192,
                               deps=(wslot_free[s],) +
                                    tuple(self.buf_tok.get("setup", ())))
            wslot_load[s] = tok
            return s

        def wtile(s, kc, mc):
            return self.aB(B_W + s * 8192 + kc * 1024 + mc * 128, 128)

        out_dma_toks = []
        for p in range(NPASS):
            tok0 = p * 1024
            in_toks = []
            if call == 0:
                stg = self.aF(A_STG, 1024)
                for tc in range(8):
                    dl = P.dma("sy_q", "stg", lambda e, tc=tc, tok0=tok0:
                               e.dma_start(
                        stg, bass.AP(self.t_x, (tok0 + tc * 128) * D,
                                     [[D, 128], [1, D]])),
                        deps=(self.buf_tok.get("ffn_stg"),) +
                             tuple(self.buf_tok.get("setup", ())))
                    last_tp = None
                    for c in range(C):
                        bk = 4 + (c % 2)
                        tp = self.transpose_into(
                            bk, stg[:, c * 128:(c + 1) * 128], 128,
                            dep=(dl, self.buf_tok.get(("accfree", c))))
                        ev = P.ve(lambda e, b=bk, c=c, tc=tc: e.tensor_copy(
                            acc(c)[:, tc * 128:(tc + 1) * 128],
                            self.pb[b][:, 0:128]), inc=True, deps=(tp,))
                        self.bank_consumed(bk, ev)
                        in_toks.append(ev)
                        last_tp = tp
                    self.buf_tok["ffn_stg"] = last_tp
                in_all = (in_toks[-1],)
                in_c = {c: in_all for c in range(C)}
            else:
                in_c = {}
                for c in range(C):
                    dl = P.dma("sy_q", f"acc{c}", lambda e, c=c, tok0=tok0:
                               e.dma_start(
                        acc(c), bass.AP(src_stream, c * 128 * T + tok0,
                                        [[T, 128], [1, 1024]])),
                        deps=tuple(self.buf_tok.get("setup", ())) +
                             (self.buf_tok.get(("accfree", c)),))
                    in_c[c] = (dl,)

            row_toks = {}
            if True:
                for g in range(2):
                    for c in range(C):
                        s1, s2 = self.stats_chunk(
                            acc(c, g), in_c[c], c == 0, c == C - 1, 6, 7, "f")
                    dmas, ta, tb5 = self.ln_rows(
                        6, 7, (s1, s2), NT, self.RS[0], tok0 + g * NT,
                        keep_ab_at=g * NT)
                    row_toks[g] = dmas
                    row_toks[(g, "B")] = tb5
                    for s in range(4):
                        tp = self.transpose_into(
                            5, self.r_a[0:1,
                                        g * NT + s * 128:g * NT + (s + 1) * 128],
                            128, dep=(ta,))
                        ev = P.ve(lambda e, gs=g * 4 + s: e.tensor_copy(
                            self.a_colT[:, gs:gs + 1], self.pb[5][0:128, 0:1]),
                            inc=True, deps=(tp,))
                        self.bank_consumed(5, ev)
                        row_toks[(g, "acol", s)] = ev

            apply_all = []
            for g in range(2):
                rs_use, col_use = self.RS[0], tok0 + g * NT
                rdep = row_toks[g] + tuple(self.buf_tok.get("setup", ())) + \
                    (self.buf_tok.get("apply_arB"),)
                for c in range(C):
                    vf, vb = self.apply_ln(
                        self.lnLTs, 1024, 128, rs_use, col_use, NT, acc(c, g),
                        None, xn(c, g), 0.0, in_c[c], rs_dep=rdep,
                        banks=(4, 5) if c % 2 else (6, 7))
                    apply_all.append(vb)
            apply_done = (apply_all[-1],)

            wrowF_tok = {}
            for g in range(2):
                dep0 = (self.bank_wait(5), self.tok("ve")) + in_c[0]
                for c in range(C):
                    P.pe(lambda e, c=c, g=g: e.matmul(
                        self.pb[5][0:NE, 0:NT],
                        self.rw_lhsT[:, c * NE:(c + 1) * NE],
                        acc(c, g), start=(c == 0), stop=False),
                        deps=dep0 if c == 0 else in_c[c])
                tcor = P.pe(lambda e, g=g: e.matmul(
                    self.pb[5][0:NE, 0:NT], self.colw_row[0:1, :],
                    self.r_b[0:1, g * NT:(g + 1) * NT],
                    start=False, stop=True), inc=True,
                    deps=(row_toks[(g, "B")],))
                cl = P.ve(lambda e: e.tensor_copy(self.lg_sb[0:NE, 0:NT],
                                                  self.pb[5][0:NE, 0:NT]),
                          inc=True, deps=(tcor,))
                self.bank_consumed(5, cl)
                wr_toks = []
                for s in range(4):
                    sl = slice(s * NE, (s + 1) * NE)
                    tp = self.transpose_into(
                        4, self.lg_sb[0:NE, s * 128:(s + 1) * 128], 128,
                        dep=(cl,))
                    st = P.ve(lambda e, sl=sl, gs=g * 4 + s: e.scalar_tensor_tensor(
                        self.lgt[:, sl], self.pb[4][:, 0:NE],
                        self.a_colT[:, gs:gs + 1],
                        self.rbias_tok[:, :], ALU.mult, ALU.add),
                        inc=True, deps=(tp, row_toks[(g, "acol", s)]))
                    self.bank_consumed(4, st)
                    pa = P.ae(lambda e, sl=sl: e.activation(
                        self.pr_t[:, sl], self.lgt[:, sl], AF.Sigmoid,
                        bias=np.float32(0.0), scale=np.float32(1.0)), inc=True, deps=(st,))
                    v1 = P.ve(lambda e, sl=sl, s=s: e.reduce_max(
                        self.m_col[:, s:s + 1], self.lgt[:, sl], AXL.X),
                        inc=True, deps=(st,))
                    v2 = P.ve(lambda e, sl=sl, s=s: e.tensor_scalar(
                        self.msk[:, sl], self.lgt[:, sl],
                        self.m_col[:, s:s + 1], None, ALU.is_ge),
                        inc=True, deps=(v1,))
                    v3 = P.ve(lambda e, sl=sl: e.scalar_tensor_tensor(
                        self.w_tok[:, sl], self.msk[:, sl], -BIG,
                        self.lgt[:, sl], ALU.mult, ALU.add),
                        inc=True, deps=(v2,))
                    v4 = P.ve(lambda e, sl=sl, s=s: e.reduce_max(
                        self.m_col[:, 4 + s:5 + s], self.w_tok[:, sl], AXL.X),
                        inc=True, deps=(v3,))
                    v5 = P.ve(lambda e, sl=sl, s=s: e.tensor_scalar(
                        self.msk[:, sl], self.lgt[:, sl],
                        self.m_col[:, 4 + s:5 + s], None, ALU.is_ge),
                        inc=True, deps=(v4,))
                    v6 = P.ve(lambda e, sl=sl: e.tensor_mul(
                        self.w_tok[:, sl], self.pr_t[:, sl], self.msk[:, sl]),
                        inc=True, deps=(v5, pa))
                    v7 = P.ve(lambda e, sl=sl, s=s: e.reduce_sum(
                        self.m_col[:, 8 + s:9 + s], self.w_tok[:, sl], AXL.X),
                        inc=True, deps=(v6,))
                    a1 = P.ve(lambda e, s=s: e.reciprocal(
                        self.m_col[:, 8 + s:9 + s],
                        self.m_col[:, 8 + s:9 + s]), inc=True, deps=(v7,))
                    v8 = P.ve(lambda e, sl=sl, s=s: e.tensor_scalar(
                        self.w_tok[:, sl], self.w_tok[:, sl],
                        self.m_col[:, 8 + s:9 + s], 0.5, ALU.mult, ALU.mult),
                        inc=True, deps=(a1, v6))
                    tp2 = self.transpose_into(4, self.w_tok[:, sl], NE,
                                              dep=(v8,))
                    ev = P.ve(lambda e, s=s, g=g: e.tensor_copy(
                        self.wrow[0:NE,
                                  g * NT + s * 128:g * NT + (s + 1) * 128],
                        self.pb[4][0:NE, 0:128]), inc=True,
                        deps=(tp2, self.buf_tok.get("wrow_free")))
                    self.bank_consumed(4, ev)
                    wr_toks.append(ev)
                wrowF_tok[g] = tuple(wr_toks)
            self.buf_tok["ra_free"] = self.tok("pe")
            for mi, (kind, tt, eoff) in enumerate(mats):
                ei = mi // 3
                s0 = load_half(mi, 0)
                s1_ = load_half(mi, 1)
                if kind in ("g", "u"):
                    for hf, s in ((0, s0), (1, s1_)):
                        last_mm = None
                        for g in range(2):
                            for mc in range(8):
                                ci = hf * 8 + mc
                                bk = mc % 2
                                dd = (self.bank_wait(bk), wslot_load[s]) + \
                                     apply_done
                                for kc in range(C):
                                    tmm = P.pe(lambda e, s=s, kc=kc, mc=mc,
                                               g=g, bk=bk: e.matmul(
                                        self.pb[bk][:, 0:NT],
                                        wtile(s, kc, mc), xn(kc, g),
                                        start=(kc == 0), stop=(kc == C - 1)),
                                        inc=(kc == C - 1),
                                        deps=dd if kc == 0 else ())
                                last_mm = tmm
                                if kind == "g":
                                    ev = P.ae(lambda e, bk=bk, ci=ci, g=g:
                                              e.activation(
                                        uu(ci, g), self.pb[bk][:, 0:NT],
                                        AF.Silu, bias=0.0, scale=1.0),
                                        inc=True,
                                        deps=(tmm,
                                              self.buf_tok.get(("ufree", ci, g))))
                                else:
                                    ev = P.ve(lambda e, bk=bk, ci=ci, g=g:
                                              e.tensor_mul(
                                        uu(ci, g), self.pb[bk][:, 0:NT],
                                        uu(ci, g)), inc=True, deps=(tmm,))
                                self.bank_consumed(bk, ev)
                                self.buf_tok[("uready", ci, g)] = ev
                        wslot_free[s] = last_mm
                else:
                    last_mm = None
                    for g in range(2):
                        if ei > 0:
                            dfw = P.dma("sc_q", "wrf", lambda e, g=g, ei=ei:
                                        e.dma_start(
                                self.wrowF[0:1, 0:NT],
                                bass.AP(self.wrow, (ei - 1) * 1024 + g * NT,
                                        [[1024, 1], [1, NT]])),
                                deps=(self.buf_tok.get("wbF"),) +
                                     wrowF_tok[g])
                            twb = P.pe(lambda e: e.matmul(
                                self.pb[4][:, 0:NT], self.ones_lhsT[0:1, :],
                                self.wrowF[0:1, 0:NT],
                                start=True, stop=True), inc=True,
                                deps=(self.bank_wait(4), dfw))
                            self.buf_tok["wbF"] = twb
                            cwb = P.ve(lambda e: e.tensor_copy(
                                self.wbs[:, 0:NT], self.pb[4][:, 0:NT]),
                                inc=True, deps=(twb,))
                            self.bank_consumed(4, cwb)
                        for mc in range(C):
                            bk = 2 + (mc % 2)
                            uready = tuple(self.buf_tok.get(("uready", kk, g))
                                           for kk in range(CI))
                            dd = (self.bank_wait(bk), wslot_load[s0],
                                  wslot_load[s1_]) + uready
                            for kk in range(CI):
                                hh, kcl = divmod(kk, 8)
                                s_ = s0 if hh == 0 else s1_
                                tmm = P.pe(lambda e, s_=s_, kcl=kcl, mc=mc,
                                           g=g, bk=bk, kk=kk: e.matmul(
                                    self.pb[bk][:, 0:NT],
                                    wtile(s_, kcl, mc), uu(kk, g),
                                    start=(kk == 0), stop=(kk == CI - 1)),
                                    inc=(kk == CI - 1),
                                    deps=dd if kk == 0 else ())
                            last_mm = tmm
                            if ei == 0:
                                ev = P.ve(lambda e, bk=bk, mc=mc, g=g:
                                          e.scalar_tensor_tensor(
                                    acc(mc, g), self.pb[bk][:, 0:NT], 0.5,
                                    acc(mc, g), ALU.mult, ALU.add),
                                    inc=True, deps=(tmm,))
                            else:
                                ev0 = P.ve(lambda e, bk=bk: e.tensor_mul(
                                    self.tA[:, 0:NT], self.pb[bk][:, 0:NT],
                                    self.wbs[:, 0:NT]), inc=True,
                                    deps=(tmm, cwb,
                                          self.buf_tok.get("ffn_tA")))
                                ev = P.ve(lambda e, mc=mc, g=g: e.tensor_add(
                                    acc(mc, g), self.tA[:, 0:NT], acc(mc, g)),
                                    inc=True, deps=(ev0,))
                                self.buf_tok["ffn_tA"] = ev
                            self.bank_consumed(bk, ev)
                    for g in range(2):
                        for kk in range(CI):
                            self.buf_tok[("ufree", kk, g)] = last_mm
                    wslot_free[s0] = last_mm
                    wslot_free[s1_] = last_mm

            last_acc = self.tok("ve")
            if call == 0:
                for g in range(2):
                    for c in range(C):
                        s1, s2 = self.stats_chunk(
                            acc(c, g), (last_acc,), c == 0, c == C - 1,
                            6, 7, "f")
                    self.ln_rows(6, 7, (s1, s2), NT, self.RS[1],
                                 tok0 + g * NT)
                dto = P.dma("sc_q", "x1o", lambda e, tok0=tok0: e.dma_start(
                    bass.AP(self.t_X1, tok0,
                            [[T, 128], [128 * T, C], [1, 1024]]),
                    self.aF(A_ACC, 8192)), deps=(last_acc,))
                out_dma_toks.append(dto)
                for c in range(C):
                    self.buf_tok[("accfree", c)] = dto
                if p == NPASS - 1:
                    guard = (self.tok("ve"), self.tok("pe"), dto)
                    self.buf_tok[("inF", 0)] = guard
                    self.buf_tok[("inF", 1)] = guard
            else:
                stg = self.aF(A_STG, 1024)
                for tc in range(8):
                    evs = []
                    for c in range(C):
                        bk = 4 + (c % 2)
                        tp = self.transpose_into(
                            bk, acc(c)[:, tc * 128:(tc + 1) * 128], 128,
                            dep=(last_acc, self.buf_tok.get("ffn_stg2")))
                        ev = P.ve(lambda e, b=bk, c=c: e.tensor_copy(
                            stg[:, c * 128:(c + 1) * 128],
                            self.pb[b][:, 0:128]), inc=True, deps=(tp,))
                        self.bank_consumed(bk, ev)
                        evs.append(ev)
                    dto = P.dma("sc_q", "outo", lambda e, tc=tc, tok0=tok0:
                                e.dma_start(
                        bass.AP(self.t_out, (tok0 + tc * 128) * D,
                                [[D, 128], [1, D]]), stg), deps=tuple(evs))
                    self.buf_tok["ffn_stg2"] = dto
                    out_dma_toks.append(dto)
                for c in range(C):
                    self.buf_tok[("accfree", c)] = out_dma_toks[-1]
        self.buf_tok["arB_free"] = (self.tok("pe"), self.tok("ve"),
                                    self.tok("ae"))
        return out_dma_toks


    def emit_attn_layer(self, li, ein, mid, xn1buf, zout, entry):
        P = self
        VTOK = 37888
        XBF, KTOK = 32768, 36864
        QBF, OBF = 36864, 40960
        XN1, USTG = 32768, 36864

        def inF(t):
            return (t % 2) * 4096

        def arb(off, n):
            return self.aB(off, n)

        dbk = P.dma("gp_q", "bk", lambda e, li=li: e.dma_start(
            self.bk_row[0:1, :], bass.AP(self.t_bk, li * D, [[D, 1], [1, D]])),
            deps=(self.buf_tok.get("bk_consumed"),))
        dbv = P.dma("gp_q", "bv", lambda e, li=li: e.dma_start(
            self.bv_row[0:1, :], bass.AP(self.t_bv, li * D, [[D, 1], [1, D]])),
            deps=(self.buf_tok.get("bv_consumed"),))

        arB_free = (self.buf_tok.get("arB_free"),)
        u0 = 48 + li * 12
        dwk = self.wb_load("aw0", u0 + 0, 0, arB_free)
        dwv = self.wb_load("aw1", u0 + 1, 8192, arB_free)
        entry_lhsT_t, entry_c0, entry_bias = entry
        entry_step = 0 if entry_bias is None else 128
        rowdep = (self.dma_last.get("d_rs_a"), self.dma_last.get("d_rs_b"),
                  self.buf_tok.get("lnLT_load"))

        mvo = P.ve(lambda e: e.memset(
            bass.AP(self.arB, VTOK + 128, [[49152, 128], [130, 8], [1, 1]]),
            1.0), inc=True, deps=arB_free + (self.buf_tok.get("ktokvtok_free"),))
        self.buf_tok["vtok_ones"] = mvo
        kv_first = [True]
        for t in range(NTILES):
            tok0 = t * NT
            base = inF(t)
            din = P.dma("sy_q", f"ain{t % 2}", lambda e, tok0=tok0, ein=ein,
                        base=base: e.dma_start(
                self.aF(base, 4096),
                bass.AP(ein, tok0, [[T, 128], [128 * T, C], [1, 512]])),
                deps=(self.buf_tok.get(("inF", t % 2)),))
            xin = lambda c, base=base: self.aF(base + c * 512, 512)
            ap_toks = []
            for c in range(C):
                eb = 0.0 if entry_bias is None else \
                    self.lnb_col[:, entry_bias + c:entry_bias + c + 1]
                vf, vb = self.apply_ln(
                    entry_lhsT_t, entry_c0 + c * entry_step, 128, self.RS[1],
                    tok0, NT, xin(c), None, arb(XBF + c * 512, 512), eb,
                    (din,), rs_dep=rowdep + (self.buf_tok.get("xinbf_free"),))
                ap_toks.append(vb)
            self.buf_tok[("inF", t % 2)] = ap_toks[-1]
            for q in range(4):
                kev = {}
                for isv, (wsl, wld, brow, banks) in enumerate(
                        ((0, dwk, self.bk_row, (0, 1)),
                         (1, dwv, self.bv_row, (2, 6)))):
                    for nh in range(2):
                        bk = banks[nh]
                        dd = (self.bank_wait(bk), wld, ap_toks[-1],
                              dbk if isv == 0 else dbv)
                        for kc in range(C):
                            tmm = P.pe(lambda e, wsl=wsl, kc=kc, nh=nh, q=q,
                                       base=base, bk=bk: e.matmul(
                                self.pb[bk][:, 0:NT],
                                self.aB(base * 0 + XBF + kc * 512 + q * 128, 128),
                                self.aB(wsl * 8192 + kc * 1024 + nh * 512, 512),
                                start=(kc == 0), stop=False),
                                deps=dd if kc == 0 else ())
                        tmm = P.pe(lambda e, brow=brow, nh=nh, bk=bk: e.matmul(
                            self.pb[bk][:, 0:NT], self.ones_lhsT[0:1, :],
                            brow[0:1, nh * 512:(nh + 1) * 512],
                            start=False, stop=True), inc=True)
                        if isv == 0:
                            ep = self.buf_tok.get("elu_prev", ())
                            a1 = P.ae(lambda e, bk=bk: e.activation(
                                self.tB[:, 0:NT], self.pb[bk][:, 0:NT],
                                AF.Relu, bias=np.float32(0.0),
                                scale=np.float32(1.0)), inc=True,
                                deps=(tmm,) + tuple(ep))
                            v1 = P.ve(lambda e, bk=bk: e.tensor_scalar(
                                self.tA[:, 0:NT], self.pb[bk][:, 0:NT],
                                0.0, None, ALU.min), inc=True,
                                deps=(tmm, a1) + tuple(ep))
                            a2 = P.ae(lambda e: e.activation(
                                self.tC[:, 0:NT], self.tA[:, 0:NT], AF.Exp,
                                bias=np.float32(0.0), scale=np.float32(1.0)),
                                inc=True, deps=(v1,))
                            v2 = P.ve(lambda e, nh=nh: e.tensor_add(
                                self.aB(KTOK + nh * 512, 512),
                                self.tB[:, 0:NT], self.tC[:, 0:NT]),
                                inc=True,
                                deps=(a2, a1,
                                      self.buf_tok.get("ktokvtok_free")))
                            self.bank_consumed(bk, v1)
                            self.buf_tok["elu_prev"] = (a2, v2)
                            kev[nh] = v2
                        else:
                            vv = P.ve(lambda e, nh=nh, bk=bk: e.tensor_copy(
                                bass.AP(self.arB, VTOK + nh * 4 * 130,
                                        [[49152, 128], [130, 4], [1, 128]]),
                                self.pb[bk][:, 0:NT]), inc=True,
                                deps=(tmm,
                                      self.buf_tok.get("ktokvtok_free")))
                            self.bank_consumed(bk, vv)
                            kev[2 + nh] = vv
                kvt = None
                for pr in range(NPAIR):
                    b = 3 + pr // 3
                    off = (pr % 3) * 130
                    dd = [kev[pr // 4], kev[2 + pr // 4]]
                    if kv_first[0]:
                        dd.append(self.bank_wait(b))
                        dd.append(self.buf_tok.get("vtok_ones"))
                    last = (t == NTILES - 1 and q == 3)
                    st_ = kv_first[0] and (pr % 3 == 0)
                    sp_ = last and (pr % 3 == 2 or pr == 7)
                    tkv = P.pe(lambda e, pr=pr, b=b, off=off, st_=st_, sp_=sp_:
                               e.matmul(
                        self.pb[b][:, off:off + 129],
                        self.aB(KTOK + pr * 128, 128),
                        bass.AP(self.arB, VTOK + pr * 130,
                                [[49152, 128], [1, 129]]),
                        start=st_, stop=sp_),
                        inc=(last or pr == 7), deps=tuple(dd))
                    if last:
                        self.buf_tok[("kv", pr)] = tkv
                    kvt = tkv
                self.buf_tok["ktokvtok_free"] = kvt
                if kv_first[0]:
                    kv_first[0] = False
            self.buf_tok["xinbf_free"] = self.tok("pe")

        self.buf_tok["aw0_free"] = self.tok("pe")
        self.buf_tok["aw1_free"] = self.tok("pe")
        zm = P.ve(lambda e: e.memset(self.zsb[:, :], 0.0), inc=True,
                  deps=(self.buf_tok.get("zsb_consumed"),))
        bd_toks = []
        for pr in range(NPAIR):
            b = 3 + pr // 3
            off = (pr % 3) * 130
            kvd = (self.buf_tok[("kv", pr)], self.buf_tok.get("BD_consumed"))
            m1 = P.ve(lambda e, pr=pr: e.memset(
                self.BD[0:64, pr * 128 + 64:pr * 128 + 128], 0.0),
                inc=True, deps=kvd)
            m2 = P.ve(lambda e, pr=pr: e.memset(
                self.BD[64:128, pr * 128:pr * 128 + 64], 0.0), inc=True)
            c1 = P.ve(lambda e, pr=pr, b=b, off=off: e.tensor_copy(
                self.BD[0:64, pr * 128:pr * 128 + 64],
                self.pb[b][0:64, off:off + 64]), inc=True)
            c2 = P.ve(lambda e, pr=pr, b=b, off=off: e.tensor_copy(
                self.BD[64:128, pr * 128 + 64:pr * 128 + 128],
                self.pb[b][64:128, off + 64:off + 128]), inc=True)
            z1 = P.ve(lambda e, pr=pr, b=b, off=off: e.tensor_copy(
                self.zsb[0:64, pr * 2:pr * 2 + 1],
                self.pb[b][0:64, off + 128:off + 129]), inc=True, deps=(zm,))
            z2 = P.ve(lambda e, pr=pr, b=b, off=off: e.tensor_copy(
                self.zsb[64:128, pr * 2 + 1:pr * 2 + 2],
                self.pb[b][64:128, off + 128:off + 129]), inc=True)
            bd_toks += [m1, m2, c1, c2, z1, z2]
            self.bank_consumed(b, z2)
        bd_done = bd_toks[-1]

        dwq = self.wb_load("aw2", u0 + 2, 2 * 8192,
                           (self.buf_tok.get("aw2_free"),))
        dwo = self.wb_load("aw3", u0 + 3, 3 * 8192,
                           (self.buf_tok.get("aw3_free"),))
        dW1pre = [self.wb_load("aw0", u0 + 4, 0, (self.buf_tok.get("aw0_free"),)),
                  self.wb_load("aw1", u0 + 5, 8192,
                               (self.buf_tok.get("aw1_free"),))]
        for t in range(NTILES):
            tok0 = t * NT
            base = inF(t)
            din = P.dma("sy_q", f"ain{t % 2}", lambda e, tok0=tok0, ein=ein,
                        base=base: e.dma_start(
                self.aF(base, 4096),
                bass.AP(ein, tok0, [[T, 128], [128 * T, C], [1, 512]])),
                deps=(self.buf_tok.get(("inF", t % 2)),))
            xin = lambda c, base=base: self.aF(base + c * 512, 512)
            ap_toks = []
            for c in range(C):
                eb = 0.0 if entry_bias is None else \
                    self.lnb_col[:, entry_bias + c:entry_bias + c + 1]
                vf, vb = self.apply_ln(
                    entry_lhsT_t, entry_c0 + c * entry_step, 128, self.RS[1],
                    tok0, NT, xin(c), xin(c), arb(XBF + c * 512, 512), eb,
                    (din,), rs_dep=rowdep + (self.buf_tok.get("xinbf_free"),),
                    banks=(4, 5) if c % 2 else (6, 7))
                ap_toks.append(vb)
            qev = []
            for mc in range(C):
                bk = mc % 2
                dd = (self.bank_wait(bk), dwq, ap_toks[-1])
                for kc in range(C):
                    tmm = P.pe(lambda e, kc=kc, mc=mc, bk=bk: e.matmul(
                        self.pb[bk][:, 0:NT],
                        self.aB(2 * 8192 + kc * 1024 + mc * 128, 128),
                        self.aB(XBF + kc * 512, 512),
                        start=(kc == 0), stop=(kc == C - 1)),
                        inc=(kc == C - 1), deps=dd if kc == 0 else ())
                bq = self.bq_col[:, li * C + mc:li * C + mc + 1]
                ep = self.buf_tok.get("elu_prev", ())
                a1 = P.ae(lambda e, bk=bk, bq=bq: e.activation(
                    self.tB[:, 0:NT], self.pb[bk][:, 0:NT], AF.Relu,
                    bias=bq, scale=np.float32(1.0)), inc=True,
                    deps=(tmm,) + tuple(ep))
                v1 = P.ve(lambda e, bk=bk, bq=bq: e.tensor_scalar(
                    self.tA[:, 0:NT], self.pb[bk][:, 0:NT], bq, 0.0,
                    ALU.add, ALU.min), inc=True, deps=(tmm, a1) + tuple(ep))
                a2 = P.ae(lambda e: e.activation(
                    self.tC[:, 0:NT], self.tA[:, 0:NT], AF.Exp,
                    bias=np.float32(0.0), scale=np.float32(1.0)),
                    inc=True, deps=(v1,))
                v2 = P.ve(lambda e, mc=mc: e.tensor_add(
                    self.aB(QBF + mc * 512, 512), self.tB[:, 0:NT],
                    self.tC[:, 0:NT]), inc=True, deps=(a2, a1))
                self.bank_consumed(bk, v1)
                self.buf_tok["elu_prev"] = (a2, v2)
                qev.append(v2)
            z_tok, vr_tok, voc_tok, oev = {}, {}, {}, {}
            zr_slot = (self.zrec[0:2, 0:NT], self.zrec2[0:2, 0:NT],
                       self.wrow[0:2, 0:NT], self.wrow[0:2, NT:2 * NT])

            def emit_zb(pr):
                bz = 4 + (pr % 4)
                zr = zr_slot[pr % 4]
                tzb = P.pe(lambda e, zr=zr, bz=bz: e.matmul(
                    self.pb[bz][:, 0:NT], self.E2[0:2, :], zr,
                    start=True, stop=True), inc=True,
                    deps=(self.bank_wait(bz), vr_tok[pr]))
                self.buf_tok[("b1r", pr % 4)] = tzb
                zbs_ = self.zbs if pr % 2 == 0 else self.wbs
                vzb = P.ve(lambda e, bz=bz, zbs_=zbs_: e.tensor_copy(
                    zbs_[:, 0:NT], self.pb[bz][:, 0:NT]), inc=True,
                    deps=(tzb,))
                self.bank_consumed(bz, vzb)
                vo = P.ve(lambda e, pr=pr, zbs_=zbs_: e.tensor_mul(
                    self.aB(OBF + pr * 512, 512), self.aB(OBF + pr * 512, 512),
                    zbs_[:, 0:NT]), inc=True, deps=(vzb, voc_tok[pr]))
                oev[pr] = vo

            for pr in range(NPAIR):
                bo_ = 2 + (pr % 2)
                bz = 4 + (pr % 4)
                to = P.pe(lambda e, pr=pr, bo_=bo_: e.matmul(
                    self.pb[bo_][:, 0:NT], self.BD[:, pr * 128:(pr + 1) * 128],
                    self.aB(QBF + pr * 512, 512), start=True, stop=True),
                    inc=True, deps=(self.bank_wait(bo_), qev[pr], bd_done))
                voc = P.ve(lambda e, pr=pr, bo_=bo_: e.tensor_copy(
                    self.aB(OBF + pr * 512, 512), self.pb[bo_][:, 0:NT]),
                    inc=True, deps=(to,))
                self.bank_consumed(bo_, voc)
                voc_tok[pr] = voc
                z_tok[pr] = P.pe(lambda e, pr=pr, bz=bz: e.matmul(
                    self.pb[bz][0:2, 0:NT], self.zsb[:, pr * 2:pr * 2 + 2],
                    self.aB(QBF + pr * 512, 512), start=True, stop=True),
                    inc=True, deps=(self.bank_wait(bz), bd_done))
                tb2 = self.tB[0:2, 0:NT] if pr % 2 == 0 else self.tB[32:34, 0:NT]
                ae1 = P.ae(lambda e, bz=bz, tb2=tb2: e.activation(
                    tb2, self.pb[bz][0:2, 0:NT], AF.Identity,
                    bias=np.float32(EPS_ATTN), scale=np.float32(1.0)),
                    inc=True, deps=(z_tok[pr], qev[-1],
                                    self.buf_tok.get(("b1z", pr % 2))))
                self.bank_consumed(bz, ae1)
                if pr >= 4:
                    emit_zb(pr - 4)
                def _vrf2(e, tb2=tb2, zr=zr_slot[pr % 4]):
                    with self.nc.allow_low_precision(
                            reason="z normalizer feeds a bf16 matmul"):
                        return e.reciprocal(zr, tb2)
                vr_tok[pr] = P.ve(_vrf2, inc=True,
                                  deps=(ae1, self.buf_tok.get(("b1r", pr % 4)),
                                        self.buf_tok.get("wrow_free")))
                self.buf_tok[("b1z", pr % 2)] = vr_tok[pr]
            for pr in range(4, NPAIR):
                emit_zb(pr)
            oev = [oev[pr] for pr in range(NPAIR)]
            self.buf_tok["elu_prev"] = (vr_tok[6], vr_tok[7])
            st_toks = []
            for mc in range(C):
                bk = mc % 2
                dd = (self.bank_wait(bk), dwo, oev[-1])
                for kc in range(C):
                    tmm = P.pe(lambda e, kc=kc, mc=mc, bk=bk: e.matmul(
                        self.pb[bk][:, 0:NT],
                        self.aB(3 * 8192 + kc * 1024 + mc * 128, 128),
                        self.aB(OBF + kc * 512, 512),
                        start=(kc == 0), stop=(kc == C - 1)),
                        inc=(kc == C - 1), deps=dd if kc == 0 else ())
                sv = P.ve(lambda e, mc=mc, bk=bk, base=base: e.scalar_tensor_tensor(
                    self.aF(base + mc * 512, 512), self.pb[bk][:, 0:NT],
                    self.bo_col[:, li * C + mc:li * C + mc + 1],
                    self.aF(base + mc * 512, 512), ALU.add, ALU.add),
                    inc=True, deps=(tmm,))
                self.bank_consumed(bk, sv)
                st_toks.append(sv)
            self.buf_tok["xinbf_free"] = self.tok("pe")
            for c in range(C):
                s1, s2 = self.stats_chunk(
                    xin(c), (st_toks[c],), c == 0, c == C - 1, 2, 3, "a")
            self.ln_rows(2, 3, (s1, s2), NT, self.RS[0], tok0)
            dout = P.dma("sc_q", "b1out", lambda e, tok0=tok0, mid=mid,
                         base=base: e.dma_start(
                bass.AP(mid, tok0, [[T, 128], [128 * T, C], [1, 512]]),
                self.aF(base, 4096)), deps=tuple(st_toks))
            self.buf_tok[("inF", t % 2)] = (dout, self.tok("ve"))
        self.buf_tok["wrow_free"] = self.tok("pe")
        self.buf_tok["BD_consumed"] = self.tok("pe")
        self.buf_tok["zsb_consumed"] = self.tok("pe")
        self.buf_tok["bk_consumed"] = self.tok("pe")
        self.buf_tok["bv_consumed"] = self.tok("pe")

        ltl = []
        pedep = (self.tok("pe"),)
        for (dcol, inst) in ((0, li), (1024, 6 + li)):
            for pb_ in (0, 32):
                ltl.append(P.dma("sy_q", "lnlt", lambda e, dcol=dcol,
                                 inst=inst, pb_=pb_: e.dma_start(
                    self.lnLT[pb_:pb_ + 3, dcol:dcol + 1024],
                    bass.AP(self.t_LT, pb_ * 14336 + inst * 1024,
                            [[14336, 3], [1, 1024]])), deps=pedep))
        dlt = dlt2 = ltl[-1]
        self.buf_tok["lnLT_load"] = tuple(ltl)
        ln1_rowdep = (self.dma_last.get("d_rs_a"), self.dma_last.get("d_rs_b"))
        dW1 = dW1pre + [
            self.wb_load(f"aw{qi}", u0 + 4 + qi, qi * 8192,
                         (self.tok("pe"),)) for qi in (2, 3)]
        for t in range(NTILES):
            tok0 = t * NT
            base = inF(t)
            din = P.dma("sy_q", f"ain{t % 2}", lambda e, tok0=tok0, mid=mid,
                        base=base: e.dma_start(
                self.aF(base, 4096),
                bass.AP(mid, tok0, [[T, 128], [128 * T, C], [1, 512]])),
                deps=(self.buf_tok.get(("inF", t % 2)),))
            xin = lambda c, base=base: self.aF(base + c * 512, 512)
            ap_toks = []
            for c in range(C):
                vf, vb = self.apply_ln(
                    self.lnLT, c * 128, 128, self.RS[0], tok0, NT,
                    xin(c), xin(c), arb(XN1 + c * 512, 512),
                    self.lnb_col[:, li * C + c:li * C + c + 1],
                    (din,), rs_dep=ln1_rowdep + (dlt, dlt2),
                    banks=(4, 5) if c % 2 else (6, 7))
                ap_toks.append(vb)
            ust = {}
            for mc in range(CF):
                bk = mc % 4
                qi = mc // 8
                dd = (self.bank_wait(bk), dW1[qi], ap_toks[-1])
                for kc in range(C):
                    tmm = P.pe(lambda e, kc=kc, mc=mc, bk=bk, qi=qi: e.matmul(
                        self.pb[bk][:, 0:NT],
                        self.aB(qi * 8192 + kc * 1024 + (mc % 8) * 128, 128),
                        self.aB(XN1 + kc * 512, 512),
                        start=(kc == 0), stop=(kc == C - 1)),
                        inc=(kc == C - 1), deps=dd if kc == 0 else ())
                g8 = mc // 8
                half = g8 % 2
                ue = P.ae(lambda e, bk=bk, mc=mc, half=half: e.activation(
                    self.aB(USTG + half * 4096 + (mc % 8) * 512, 512),
                    self.pb[bk][:, 0:NT], AF.Relu,
                    bias=self.b1_col[:, li * CF + mc:li * CF + mc + 1],
                    scale=np.float32(1.0)), inc=True,
                    deps=(tmm, self.buf_tok.get(("ustg", half))))
                self.bank_consumed(bk, ue)
                ust.setdefault(g8, []).append(ue)
                if mc % 8 == 7:
                    du = P.dma("sc_q", "uso", lambda e, g8=g8, half=half,
                               tok0=tok0: e.dma_start(
                        bass.AP(self.t_US, g8 * 1024 * T + tok0,
                                [[T, 128], [128 * T, 8], [1, 512]]),
                        self.aB(USTG + half * 4096, 4096)),
                        deps=tuple(ust[g8]))
                    self.buf_tok[("ustg", half)] = du
            dxo = P.dma("sc_q", "b2aout", lambda e, tok0=tok0, base=base,
                        xn1buf=xn1buf: e.dma_start(
                bass.AP(xn1buf, tok0, [[T, 128], [128 * T, C], [1, 512]]),
                self.aF(base, 4096)), deps=tuple(a for a in ap_toks))
            self.buf_tok[("inF", t % 2)] = (dxo,)
        self.buf_tok["aw2_free"] = self.tok("pe")
        self.buf_tok["aw3_free"] = self.tok("pe")

        dW2 = []
        for qi in range(4):
            dW2.append(self.wb_load(f"aw{qi}", u0 + 8 + qi, qi * 8192,
                                    (self.tok("pe"),)))
        for t in range(NTILES):
            tok0 = t * NT
            base = inF(t)
            din = P.dma("sy_q", f"ain{t % 2}", lambda e, tok0=tok0, base=base,
                        xn1buf=xn1buf: e.dma_start(
                self.aF(base, 4096),
                bass.AP(xn1buf, tok0, [[T, 128], [128 * T, C], [1, 512]])),
                deps=(self.buf_tok.get(("inF", t % 2)),))
            xin = lambda c, base=base: self.aF(base + c * 512, 512)
            duh = []
            for kh in range(2):
                duh.append(P.dma("sy_q", f"uh{kh}", lambda e, kh=kh, tok0=tok0:
                                 e.dma_start(
                    self.aB(32768 + kh * 8192, 8192),
                    bass.AP(self.t_US, kh * 2048 * T + tok0,
                            [[T, 128], [128 * T, 16], [1, 512]])),
                    deps=(self.buf_tok.get(("uhf", kh)),
                          self.dma_last.get("d_uso") if t == 0 else None,
                          self.tok("pe") if t == 0 else None)))
            first_mm = {}
            for mc in range(C):
                for kc in range(32):
                    kh = kc // 16
                    j = kc % 16
                    qi = kc // 8
                    dd = [duh[kh], dW2[qi]]
                    if kc == 0:
                        dd.append(self.bank_wait(mc))
                    do_inc = (kc == 31)
                    tmm = P.pe(lambda e, qi=qi, kc=kc, j=j, kh=kh, mc=mc:
                               e.matmul(
                        self.pb[mc][:, 0:NT],
                        self.aB(qi * 8192 + (kc % 8) * 1024 + mc * 128, 128),
                        self.aB(32768 + kh * 8192 + j * 512, 512),
                        start=(kc == 0), stop=(kc == 31)),
                        inc=do_inc, deps=tuple(dd) if kc % 8 == 0 else ())
                    if kc == 31:
                        first_mm[mc] = tmm
            self.buf_tok[("uhf", 0)] = first_mm[C - 1]
            self.buf_tok[("uhf", 1)] = first_mm[C - 1]
            st_toks = []
            for mc in range(C):
                sv = P.ve(lambda e, mc=mc, base=base: e.scalar_tensor_tensor(
                    self.aF(base + mc * 512, 512), self.pb[mc][:, 0:NT],
                    self.b2_col[:, li * C + mc:li * C + mc + 1],
                    self.aF(base + mc * 512, 512), ALU.add, ALU.add),
                    inc=True, deps=(first_mm[mc], din))
                self.bank_consumed(mc, sv)
                st_toks.append(sv)
            for c in range(C):
                s1, s2 = self.stats_chunk(
                    xin(c), (st_toks[c],), c == 0, c == C - 1, 6, 7, "a")
            self.ln_rows(6, 7, (s1, s2), NT, self.RS[1], tok0)
            dzo = P.dma("sc_q", "b2bout", lambda e, tok0=tok0, base=base,
                        zout=zout: e.dma_start(
                bass.AP(zout, tok0, [[T, 128], [128 * T, C], [1, 512]]),
                self.aF(base, 4096)), deps=tuple(st_toks))
            self.buf_tok[("inF", t % 2)] = (dzo, self.tok("ve"))
        self.buf_tok["arB_free"] = self.tok("pe")
        self.buf_tok["aw2_free2"] = self.tok("pe")
        self.buf_tok["lnLT_consumed"] = self.tok("pe")

    def emit_final(self, zin, x1buf, x2out):
        P = self
        rowdep2 = (self.dma_last.get("d_rs_a"), self.dma_last.get("d_rs_b"))
        for t in range(NTILES):
            tok0 = t * NT
            base = (t % 2) * 4096
            din = P.dma("sy_q", f"ain{t % 2}", lambda e, tok0=tok0, base=base,
                        zin=zin: e.dma_start(
                self.aF(base, 4096),
                bass.AP(zin, tok0, [[T, 128], [128 * T, C], [1, 512]])),
                deps=(self.buf_tok.get(("inF", t % 2)),))
            xin = lambda c, base=base: self.aF(base + c * 512, 512)
            last = None
            for c in range(C):
                vf, vb = self.apply_ln(
                    self.lnLT, 1024 + c * 128, 128, self.RS[1], tok0, NT,
                    xin(c), xin(c), None,
                    self.lnb_col[:, (6 + 5) * C + c:(6 + 5) * C + c + 1],
                    (din,), rs_dep=rowdep2, banks=(4, 5))
                s1, s2 = self.stats_chunk(
                    xin(c), (vf,), c == 0, c == C - 1, 6, 7, "a")
                last = vf
            self.ln_rows(6, 7, (s1, s2), NT, self.RS[0], tok0)
            self.buf_tok[("inF", t % 2)] = (last, self.tok("ve"),
                                            self.tok("pe"))
        lnf_rowdep = (self.dma_last.get("d_rs_a"), self.dma_last.get("d_rs_b"))
        for t in range(NTILES):
            tok0 = t * NT
            base = (t % 2) * 4096
            din = P.dma("sy_q", f"ain{t % 2}", lambda e, tok0=tok0, base=base,
                        zin=zin: e.dma_start(
                self.aF(base, 4096),
                bass.AP(zin, tok0, [[T, 128], [128 * T, C], [1, 512]])),
                deps=(self.buf_tok.get(("inF", t % 2)),))
            xin = lambda c, base=base: self.aF(base + c * 512, 512)
            x1base = ((t + 1) % 2) * 4096
            dx1 = P.dma("sy_q", f"ain{(t + 1) % 2}", lambda e, tok0=tok0,
                        x1base=x1base, x1buf=x1buf: e.dma_start(
                self.aF(x1base, 4096),
                bass.AP(x1buf, tok0, [[T, 128], [128 * T, C], [1, 512]])),
                deps=(self.buf_tok.get(("inF", (t + 1) % 2)),))
            toks = []
            for c in range(C):
                vf, _ = self.apply_ln(
                    self.lnLT, 1024 + c * 128, 128, self.RS[1], tok0, NT,
                    xin(c), xin(c), None,
                    self.lnb_col[:, 11 * C + c:11 * C + c + 1],
                    (din,), rs_dep=rowdep2)
                vf2, _ = self.apply_ln(
                    self.lnLTs, c * 128, 128, self.RS[0], tok0, NT,
                    xin(c), xin(c), None,
                    self.lnb_col[:, 12 * C + c:12 * C + c + 1],
                    (vf,), rs_dep=lnf_rowdep, banks=(4, 5))
                va = P.ve(lambda e, c=c, base=base, x1base=x1base: e.tensor_add(
                    self.aF(base + c * 512, 512),
                    self.aF(x1base + c * 512, 512),
                    self.aF(base + c * 512, 512)), inc=True, deps=(vf2, dx1))
                toks.append(va)
            dxo = P.dma("sc_q", "f2out", lambda e, tok0=tok0, base=base,
                        x2out=x2out: e.dma_start(
                bass.AP(x2out, tok0, [[T, 128], [128 * T, C], [1, 512]]),
                self.aF(base, 4096)), deps=tuple(toks))
            self.buf_tok[("inF", t % 2)] = dxo
            self.buf_tok[("inF", (t + 1) % 2)] = dxo
        for c in range(C):
            self.buf_tok[("accfree", c)] = (self.tok("ve"),
                                            self.dma_last.get("d_f2out"))

    def build(self):
        self.emit_setup()
        self.emit_precast()
        out_toks = self.emit_ffn(call=0)
        if self.debug_stop == "ffn1":
            self.finish(out_toks, debug_src=self.t_X1)
            return
        mac = (self.lnLTs, 1024, None)
        bufs = [self.t_S0, self.t_S1]
        zout = None
        for li in range(NLAYER):
            if li == 0:
                ein, mid, xn1b, zout = self.t_X1, self.t_S1, self.t_S0, self.t_S1
                entry = mac
            else:
                ein = zout
                mid = bufs[0] if ein is bufs[1] else bufs[1]
                xn1b, zout = ein, mid
                entry = (self.lnLT, 1024, (6 + li - 1) * C)
            self.emit_attn_layer(li, ein, mid, xn1b, zout, entry)
            if self.debug_stop == f"layer{li}":
                self.finish(out_toks, debug_src=zout)
                return
        x2buf = bufs[0] if zout is bufs[1] else bufs[1]
        self.emit_final(zout, self.t_X1, x2buf)
        if self.debug_stop == "f2":
            self.finish(out_toks, debug_src=x2buf)
            return
        out_toks = self.emit_ffn(call=1, src_stream=x2buf)
        self.finish(out_toks)

    def finish(self, out_toks, debug_src=None):
        P = self
        if debug_src is not None:
            alldma = tuple(self.dma_last.values())
            d = P.dma("sy_q", "dbg", lambda e: e.dma_start(
                bass.AP(self.t_out, 0, [[T * D, 1], [1, T * D]]),
                bass.AP(debug_src, 0, [[T * D, 1], [1, T * D]])),
                deps=tuple(out_toks) + alldma +
                     (self.tok("ve"), self.tok("pe"), self.tok("ae")))
        for kname, tok in list(self.dma_last.items()):
            self.wait("sync", tok)
        block = self.ctx.enter_context(self.nc.Block())
        for eng in ENGS:
            def _section(e, eng=eng):
                for f in self.ops[eng]:
                    f(e)
            getattr(block, eng)(_section)


def build_nc(debug_stop=None):
    nc = bass.Bass("TRN2", target_bir_lowering=False)
    ctx = contextlib.ExitStack()
    with ctx:
        mk = MK(nc, ctx, debug_stop=debug_stop)
        mk.build()
    return nc


def kernel(**inputs):
    nc = build_nc()
    ident = np.eye(128, dtype=np.float32)
    x = np.ascontiguousarray(inputs["L0_attn_x"])
    in_maps = []
    for c in range(8):
        m = {k: np.ascontiguousarray(v) for k, v in inputs.items()}
        m["L0_attn_x"] = np.ascontiguousarray(x[c])
        m["ident"] = ident
        in_maps.append(m)
    res = run_bass_kernel_spmd(nc, in_maps, core_ids=list(range(8)))
    return np.stack([r["out"] for r in res.results], axis=0)


# version 12
# speedup vs baseline: 1.0371x; 1.0371x over previous
import contextlib

import numpy as np
import ml_dtypes

import concourse.bass as bass
import concourse.mybir as mybir
from concourse.bass_utils import run_bass_kernel_spmd

F32 = mybir.dt.float32
BF16 = mybir.dt.bfloat16
AF = mybir.ActivationFunctionType
ALU = mybir.AluOpType
AXL = mybir.AxisListType

T = 4096
D = 1024
C = 8
DFF = 4096
CF = 32
DI = 2048
CI = 16
NPAIR = 8
NLAYER = 6
NE = 7
NT = 512
NTILES = 8
NPASS = 4
EPS_LN = 1e-5
EPS_ATTN = 1e-6
BIG = 1.0e9

ENGS = ("tensor", "vector", "scalar", "sync", "gpsimd")
Q2E = {"sy_q": "sync", "sc_q": "scalar", "gp_q": "gpsimd"}


class MK:
    def __init__(self, nc, ctx, debug_stop=None):
        self.nc = nc
        self.ctx = ctx
        self.debug_stop = debug_stop
        self.ops = {e: [] for e in ENGS}
        self.sems = {}
        self.counts = {}
        for name in ("pe", "ve", "ae"):
            self.sems[name] = ctx.enter_context(nc.semaphore(name))
            self.counts[name] = 0
        self.last_wait = {e: {} for e in ENGS}
        self.dma_last = {}
        self.declare_tensors()
        self.bank_free = [None] * 8
        self.buf_tok = {}

    def tok(self, sem):
        return (sem, self.counts[sem])

    def wait(self, eng, token):
        if token is None:
            return
        if isinstance(token, (list, tuple)) and token and \
                not isinstance(token[0], str):
            for t in token:
                self.wait(eng, t)
            return
        sem, val = token
        if val <= 0 or self.last_wait[eng].get(sem, 0) >= val:
            return
        self.last_wait[eng][sem] = val
        h = self.sems[sem]
        self.ops[eng].append(lambda e, h=h, v=val: e.wait_ge(h, v))

    def emit(self, eng, fn, inc=None, inc_amt=1, deps=()):
        for t in deps:
            self.wait(eng, t)
        token = None
        if inc is not None:
            self.counts[inc] += inc_amt
            token = (inc, self.counts[inc])
            h = self.sems[inc]
            self.ops[eng].append(
                lambda e, fn=fn, h=h, amt=inc_amt: fn(e).then_inc(h, amt))
        else:
            self.ops[eng].append(fn)
        return token

    def pe(self, fn, inc=False, deps=()):
        return self.emit("tensor", fn, "pe" if inc else None, deps=deps)

    def ve(self, fn, inc=False, deps=()):
        return self.emit("vector", fn, "ve" if inc else None, deps=deps)

    def ae(self, fn, inc=False, deps=()):
        return self.emit("scalar", fn, "ae" if inc else None, deps=deps)

    def dma(self, queue, key, fn, deps=()):
        eng = Q2E[queue]
        kname = "d_" + key
        if kname not in self.sems:
            self.sems[kname] = self.ctx.enter_context(self.nc.semaphore(kname))
            self.counts[kname] = 0
        prev = self.dma_last.get(kname)
        dd = tuple(deps)
        if prev is not None:
            dd = dd + (prev,)
        tok = self.emit(eng, fn, kname, inc_amt=16, deps=dd)
        self.dma_last[kname] = tok
        return tok

    def bank_wait(self, i):
        return self.bank_free[i]

    def bank_consumed(self, i, token):
        self.bank_free[i] = token

    def declare_tensors(self):
        nc, E_ = self.nc, self.ctx.enter_context
        dt = nc.dram_tensor
        self.t_x = dt("L0_attn_x", [T, D], F32, kind="ExternalInput")
        self.t_Wq = dt("L0_attn_Wq", [NLAYER, D, D], F32, kind="ExternalInput")
        self.t_bq = dt("L0_attn_bq", [NLAYER, D], F32, kind="ExternalInput")
        self.t_Wk = dt("L0_attn_Wk", [NLAYER, D, D], F32, kind="ExternalInput")
        self.t_bk = dt("L0_attn_bk", [NLAYER, D], F32, kind="ExternalInput")
        self.t_Wv = dt("L0_attn_Wv", [NLAYER, D, D], F32, kind="ExternalInput")
        self.t_bv = dt("L0_attn_bv", [NLAYER, D], F32, kind="ExternalInput")
        self.t_Wo = dt("L0_attn_Wo", [NLAYER, D, D], F32, kind="ExternalInput")
        self.t_bo = dt("L0_attn_bo", [NLAYER, D], F32, kind="ExternalInput")
        self.t_ln1w = dt("L0_attn_ln1_w", [NLAYER, D], F32, kind="ExternalInput")
        self.t_ln1b = dt("L0_attn_ln1_b", [NLAYER, D], F32, kind="ExternalInput")
        self.t_W1 = dt("L0_attn_W1", [NLAYER, D, DFF], F32, kind="ExternalInput")
        self.t_b1 = dt("L0_attn_b1", [NLAYER, DFF], F32, kind="ExternalInput")
        self.t_W2 = dt("L0_attn_W2", [NLAYER, DFF, D], F32, kind="ExternalInput")
        self.t_b2 = dt("L0_attn_b2", [NLAYER, D], F32, kind="ExternalInput")
        self.t_ln2w = dt("L0_attn_ln2_w", [NLAYER, D], F32, kind="ExternalInput")
        self.t_ln2b = dt("L0_attn_ln2_b", [NLAYER, D], F32, kind="ExternalInput")
        self.t_lnfw = dt("L0_attn_lnf_w", [D], F32, kind="ExternalInput")
        self.t_lnfb = dt("L0_attn_lnf_b", [D], F32, kind="ExternalInput")
        self.t_sg = dt("L0_ffn_shared_gate_w", [D, DI], F32, kind="ExternalInput")
        self.t_su = dt("L0_ffn_shared_up_w", [D, DI], F32, kind="ExternalInput")
        self.t_sd = dt("L0_ffn_shared_down_w", [DI, D], F32, kind="ExternalInput")
        self.t_rg = dt("L0_ffn_routed_gate_w", [NE, D, DI], F32, kind="ExternalInput")
        self.t_ru = dt("L0_ffn_routed_up_w", [NE, D, DI], F32, kind="ExternalInput")
        self.t_rd = dt("L0_ffn_routed_down_w", [NE, DI, D], F32, kind="ExternalInput")
        self.t_rw = dt("L0_ffn_router_w", [D, NE], F32, kind="ExternalInput")
        self.t_rb = dt("L0_ffn_routing_bias", [NE], F32, kind="ExternalInput")
        self.t_id = dt("ident", [128, 128], F32, kind="ExternalInput")
        self.t_out = dt("out", [T, D], F32, kind="ExternalOutput")
        self.t_S0 = dt("S0", [D, T], F32)
        self.t_S1 = dt("S1", [D, T], F32)
        self.t_X1 = dt("X1", [D, T], F32)
        self.t_US = dt("US", [DFF, T], BF16)
        self.t_LT = dt("LTd", [35, 14336], BF16)
        self.t_WB = dt("WB", [120, 128, 8192], BF16)

        st, pt = nc.sbuf_tensor, nc.psum_tensor
        self.pb = [E_(pt(f"pb{i}", [128, 512], F32)) for i in range(8)]
        self.ident = E_(st("identt", [128, 128], F32))
        self.ones_lhsT = E_(st("ones_lhsT", [1, 128], BF16))
        self.ones_lhsT_f = E_(st("ones_lhsT_f", [1, 128], F32))
        self.ones_col = E_(st("ones_col", [128, 1], BF16))
        self.E2 = E_(st("E2", [2, 128], BF16))
        self.lnLT = E_(st("lnLT", [35, 2048], BF16))
        self.lnLTs = E_(st("lnLTs", [35, 1152], BF16))
        self.bq_col = E_(st("bq_col", [128, NLAYER * C], F32))
        self.bo_col = E_(st("bo_col", [128, NLAYER * C], F32))
        self.b2_col = E_(st("b2_col", [128, NLAYER * C], F32))
        self.b1_col = E_(st("b1_col", [128, NLAYER * CF], F32))
        self.lnb_col = E_(st("lnb_col", [128, 13 * C], F32))
        self.rw_lhsT = E_(st("rw_lhsT", [128, C * NE], F32))
        self.colw_row = E_(st("colw_row", [1, NE], F32))
        self.rbias_row = E_(st("rbias_row", [1, NE], F32))
        self.rbias_tok = E_(st("rbias_tok", [128, NE], F32))
        self.RS = [E_(st(f"RS{i}", [35, T], BF16)) for i in range(2)]
        self.r_a = E_(st("r_a", [1, 1024], F32))
        self.r_b = E_(st("r_b", [1, 1024], F32))
        self.r_t = E_(st("r_t", [1, NT], F32))
        self.rowstage = E_(st("rowstage", [1, 3072], BF16))
        self.BD = E_(st("BD", [128, NPAIR * 128], BF16))
        self.zsb = E_(st("zsb", [128, NPAIR * 2], BF16))
        self.zrec = E_(st("zrec", [2, NT], BF16))
        self.zrec2 = E_(st("zrec2", [2, NT], BF16))
        self.bk_row = E_(st("bk_row", [1, D], BF16))
        self.bv_row = E_(st("bv_row", [1, D], BF16))
        self.zbs = E_(st("zbs", [128, NT], F32))
        self.wbs = E_(st("wbs", [128, NT], F32))
        self.lg_sb = E_(st("lg_sb", [7, NT], F32))
        self.lgt = E_(st("lgt", [128, 4 * NE], F32))
        self.pr_t = E_(st("pr_t", [128, 4 * NE], F32))
        self.msk = E_(st("msk", [128, 4 * NE], F32))
        self.m_col = E_(st("m_col", [128, 12], F32))
        self.w_tok = E_(st("w_tok", [128, 4 * NE], F32))
        self.a_colT = E_(st("a_colT", [128, 8], F32))
        self.wrow = E_(st("wrow", [7, 1024], BF16))
        self.wrowF = E_(st("wrowF", [1, 7 * NT], BF16))
        self.xbf = [E_(st(f"xbf{i}", [128, NT], BF16)) for i in range(2)]
        self.xsq = E_(st("xsq", [128, NT], BF16))
        self.tA = E_(st("tA", [128, NT], F32))
        self.tB = E_(st("tB", [128, NT], F32))
        self.tC = E_(st("tC", [128, NT], F32))
        self.arF = E_(st("arF", [128, 9 * 1024], F32))
        self.arB = E_(st("arB", [128, 48 * 1024], BF16))

    def aF(self, off, n):
        return self.arF[:, off:off + n]

    def aB(self, off, n):
        return self.arB[:, off:off + n]

    def transpose_into(self, bank, in_ap, n, dep=()):
        k = in_ap.shape[0]
        pbt = self.pb[bank]
        dd = (self.bank_wait(bank),) + tuple(dep)
        return self.pe(
            lambda e, b=pbt, i=in_ap, idn=self.ident[0:k, 0:k], n=n, k=k:
                e.transpose(b[0:n, 0:k], i, idn),
            inc=True, deps=dd)

    def stats_chunk(self, src_f32, deps, first, last, sum_bank, sq_bank, key):
        P = self
        n = src_f32.shape[-1]
        i = self.buf_tok.get((key, "xbfi"), 0)
        xb = self.xbf[i][:, 0:n]
        cv = P.ve(lambda e: e.tensor_copy(xb, src_f32), inc=True,
                  deps=tuple(deps) + self.buf_tok.get((key, "xbf", i), ()))
        cq = P.ae(lambda e: e.activation(self.xsq[:, 0:n], xb, AF.Square,
                                         bias=np.float32(0.0), scale=np.float32(1.0)),
                  inc=True, deps=(cv, self.buf_tok.get((key, "xsq"))))
        mm_deps = [cv]
        if first:
            mm_deps += [self.bank_wait(sum_bank), self.bank_wait(sq_bank)]
        t1 = P.pe(lambda e: e.matmul(self.pb[sum_bank][0:1, 0:n],
                                     self.ones_col[:, :], xb,
                                     start=first, stop=last),
                  inc=True, deps=tuple(mm_deps))
        t2 = P.pe(lambda e: e.matmul(self.pb[sq_bank][0:1, 0:n],
                                     self.ones_col[:, :], self.xsq[:, 0:n],
                                     start=first, stop=last),
                  inc=True, deps=(cq,))
        self.buf_tok[(key, "xbf", i)] = (t1, cq)
        self.buf_tok[(key, "xsq")] = t2
        self.buf_tok[(key, "xbfi")] = 1 - i
        return t1, t2

    def ln_rows(self, sum_bank, sq_bank, pe_toks, n, rs, rs_col, keep_ab_at=None):
        P = self
        pbS, pbQ = self.pb[sum_bank], self.pb[sq_bank]
        if keep_ab_at is not None:
            ra = self.r_a[0:1, keep_ab_at:keep_ab_at + n]
            rb = self.r_b[0:1, keep_ab_at:keep_ab_at + n]
        else:
            ra = self.r_a[0:1, 0:n]
            rb = self.r_b[0:1, 0:n]
        rt = self.r_t[0:1, 0:n]
        t1 = P.ae(lambda e: e.activation(rt, pbS[0:1, 0:n], AF.Square,
                                         bias=np.float32(0.0), scale=np.float32(1.0 / D)),
                  inc=True,
                  deps=tuple(pe_toks) + (self.buf_tok.get("ln_rt"),
                                         self.buf_tok.get("ra_free"),))
        t2 = P.ve(lambda e: e.scalar_tensor_tensor(
            rt, pbQ[0:1, 0:n], 1.0 / D, rt, ALU.mult, ALU.subtract),
            inc=True, deps=tuple(pe_toks) + (t1,))
        t3a = P.ae(lambda e: e.activation(rt, rt, AF.Sqrt,
                                          bias=np.float32(EPS_LN), scale=np.float32(1.0)),
                   inc=True, deps=(t2,))
        t3 = P.ve(lambda e: e.reciprocal(ra, rt), inc=True, deps=(t3a,))
        t4 = P.ae(lambda e: e.activation(rt, pbS[0:1, 0:n], AF.Identity,
                                         bias=np.float32(0.0), scale=np.float32(-1.0 / D)),
                  inc=True, deps=(t2, t3))
        self.bank_consumed(sum_bank, t4)
        self.bank_consumed(sq_bank, t2)
        t5 = P.ve(lambda e: e.tensor_mul(rb, rt, ra), inc=True, deps=(t3, t4))
        rs_ = self.rowstage
        rsdep = tuple(self.buf_tok.get("rowstage", ()))
        tb = self.tB[0:1, 0:n]
        tbdep = (self.buf_tok.get("ln_tb"),)
        c1 = P.ve(lambda e: e.tensor_copy(rs_[0:1, 0:n], ra), inc=True,
                  deps=(t3,) + rsdep)
        c2 = P.ve(lambda e: e.tensor_copy(rs_[0:1, 2 * n:3 * n], ra), inc=True,
                  deps=(t3,))
        c3 = P.ve(lambda e: e.tensor_copy(tb, rs_[0:1, 0:n]), inc=True,
                  deps=(c1,) + tbdep)
        c4 = P.ve(lambda e: e.tensor_sub(rs_[0:1, n:2 * n], ra, tb), inc=True,
                  deps=(c3,))
        c5 = P.ve(lambda e: e.tensor_copy(rs_[0:1, 3 * n:4 * n], rb), inc=True,
                  deps=(t5,))
        c6 = P.ve(lambda e: e.tensor_copy(rs_[0:1, 5 * n:6 * n], rb), inc=True,
                  deps=(t5,))
        c7 = P.ve(lambda e: e.tensor_copy(tb, rs_[0:1, 3 * n:4 * n]), inc=True,
                  deps=(c5, c4))
        c8 = P.ve(lambda e: e.tensor_sub(rs_[0:1, 4 * n:5 * n], rb, tb),
                  inc=True, deps=(c7,))
        self.buf_tok["ln_tb"] = c8
        d1 = P.dma("sc_q", "rs_a", lambda e: e.dma_start(
            bass.AP(rs, rs_col, [[T, 3], [1, n]]),
            rs_[0:1, 0:3 * n]), deps=(c2, c4))
        d2 = P.dma("sc_q", "rs_b", lambda e: e.dma_start(
            bass.AP(rs, 32 * T + rs_col, [[T, 3], [1, n]]),
            rs_[0:1, 3 * n:6 * n]), deps=(c6, c8))
        self.buf_tok["rowstage"] = (d1, d2)
        self.buf_tok["ln_rt"] = t5
        return (d1, d2), t3, t5

    def apply_ln(self, lhsT_t, lhsT_c0, lhsT_w, rs, rs_col, n, src_ap,
                 dst_f32, dst_bf, bias_col, deps, rs_dep=(), banks=(6, 7)):
        P = self
        bA, bB = banks
        rsA = rs[0:3, rs_col:rs_col + n]
        rsB = rs[32:35, rs_col:rs_col + n]
        lhA = lhsT_t[0:3, lhsT_c0:lhsT_c0 + lhsT_w]
        lhB = lhsT_t[32:35, lhsT_c0:lhsT_c0 + lhsT_w]
        tA_ = P.pe(lambda e: e.matmul(self.pb[bA][:, 0:n], lhA, rsA,
                                      start=True, stop=True),
                   inc=True,
                   deps=(self.bank_wait(bA),) + tuple(rs_dep))
        tB_ = P.pe(lambda e: e.matmul(self.pb[bB][:, 0:n], lhB, rsB,
                                      start=True, stop=True),
                   inc=True, deps=(self.bank_wait(bB),))
        tc = self.tC[:, 0:n]
        v1 = P.ve(lambda e: e.tensor_mul(tc, src_ap, self.pb[bA][:, 0:n]),
                  inc=True, deps=(tA_, self.buf_tok.get("apply_tC")) +
                  tuple(deps))
        self.bank_consumed(bA, v1)
        dst1 = dst_f32 if dst_f32 is not None else dst_bf
        v2 = P.ve(lambda e: e.scalar_tensor_tensor(
            dst1, tc, bias_col, self.pb[bB][:, 0:n], ALU.add, ALU.add),
            inc=True, deps=(tB_, v1))
        self.bank_consumed(bB, v2)
        self.buf_tok["apply_tC"] = v2
        v3 = None
        if dst_f32 is not None and dst_bf is not None:
            v3 = P.ve(lambda e: e.tensor_copy(dst_bf, dst_f32), inc=True,
                      deps=(v2,))
        return v2, (v3 if v3 is not None else v2)

    def emit_precast(self):
        P = self
        self.wb_tok = {}
        units = []
        ffn_mats = [("gu", self.t_sg, 0), ("gu", self.t_su, 0),
                    ("d", self.t_sd, 0)]
        for e in range(NE):
            ffn_mats += [("gu", self.t_rg, e * D * DI),
                         ("gu", self.t_ru, e * D * DI),
                         ("d", self.t_rd, e * DI * D)]
        for mi, (kind, tt, eoff) in enumerate(ffn_mats):
            for h in range(2):
                if kind == "gu":
                    src = bass.AP(tt, eoff + h * 1024,
                                  [[DI, 128], [128 * DI, C], [1, 1024]])
                else:
                    src = bass.AP(tt, eoff + h * 1024 * D,
                                  [[D, 128], [128 * D, C], [1, 1024]])
                units.append((mi * 2 + h, src))
        for li in range(NLAYER):
            u0 = 48 + li * 12
            for j, tt in enumerate((self.t_Wk, self.t_Wv, self.t_Wq, self.t_Wo)):
                units.append((u0 + j, bass.AP(
                    tt, li * D * D, [[D, 128], [128 * D, C], [1, 1024]])))
            for qi in range(4):
                units.append((u0 + 4 + qi, bass.AP(
                    self.t_W1, li * D * DFF + qi * 1024,
                    [[DFF, 128], [128 * DFF, C], [1, 1024]])))
            for qi in range(4):
                units.append((u0 + 8 + qi, bass.AP(
                    self.t_W2, li * DFF * D + qi * 1024 * D,
                    [[D, 128], [128 * D, C], [1, 1024]])))
        for u, src in units:
            dst = bass.AP(self.t_WB, u * 128 * 8192,
                          [[8192, 128], [1024, 8], [1, 1024]])
            self.wb_tok[u] = P.dma("gp_q", "pc", lambda e, dst=dst, src=src:
                                   e.dma_start(dst, src))

    def wb_load(self, key, unit, arb_off, deps=()):
        src = bass.AP(self.t_WB, unit * 128 * 8192, [[8192, 128], [1, 8192]])
        return self.dma("sc_q", key, lambda e, src=src, arb_off=arb_off:
                        e.dma_start(self.aB(arb_off, 8192), src),
                        deps=tuple(deps) + (self.wb_tok[unit],))

    def emit_setup(self):
        P = self
        d_id = P.dma("sy_q", "id", lambda e: e.dma_start(self.ident[:, :],
                                                   self.t_id.ap()))
        P.ve(lambda e: e.memset(self.ones_lhsT[:, :], 1.0))
        P.ve(lambda e: e.memset(self.ones_lhsT_f[:, :], 1.0))
        P.ve(lambda e: e.memset(self.ones_col[:, :], 1.0))
        P.ve(lambda e: e.memset(self.rowstage[0:1, 0:64], 1.0))
        P.ve(lambda e: e.memset(self.rowstage[0:1, 64:192], 0.0))
        m1 = P.ve(lambda e: e.memset(self.rowstage[0:1, 192:256], 1.0), inc=True)
        dE2 = P.dma("sy_q", "st", lambda e: e.dma_start(
            bass.AP(self.E2, 0, [[128, 2], [1, 128]]),
            self.rowstage[0:1, 0:256]), deps=(m1,))
        self.buf_tok["rowstage"] = (dE2,)

        stage = self.arF
        jobs = [
            (self.t_bq, D, NLAYER, self.bq_col, C, 0),
            (self.t_bo, D, NLAYER, self.bo_col, C, 0),
            (self.t_b2, D, NLAYER, self.b2_col, C, 0),
            (self.t_b1, DFF, NLAYER, self.b1_col, CF, 0),
            (self.t_ln1b, D, NLAYER, self.lnb_col, C, 0),
            (self.t_ln2b, D, NLAYER, self.lnb_col, C, 6 * C),
            (self.t_lnfb, D, 1, self.lnb_col, C, 12 * C),
        ]
        prev = d_id
        for (tt, width, nl, col_t, nch, base) in jobs:
            dld = P.dma("sy_q", "st", lambda e, tt=tt, nl=nl, width=width: e.dma_start(
                stage[0:nl, 0:width],
                bass.AP(tt, 0, [[width, nl], [1, width]])), deps=(prev,))
            for ch in range(nch):
                bk = 2 + (ch % 6)
                tp = self.transpose_into(
                    bk, stage[0:nl, ch * 128:(ch + 1) * 128], 128, dep=(dld,))
                ev = P.ve(lambda e, b=bk, nl=nl, col_t=col_t, base=base,
                          ch=ch, nch=nch: e.tensor_copy(
                    bass.AP(col_t, base + ch,
                            [[col_t.shape[1], 128], [nch, nl]]),
                    self.pb[b][0:128, 0:nl]), inc=True, deps=(tp,))
                self.bank_consumed(bk, ev)
            prev = ev

        drw = P.dma("sy_q", "st", lambda e: e.dma_start(
            self.rw_lhsT[:, :],
            bass.AP(self.t_rw, 0, [[NE, 128], [128 * NE, C], [1, NE]])))
        drb = P.dma("sy_q", "st", lambda e: e.dma_start(
            self.rbias_row[0:1, :], bass.AP(self.t_rb, 0, [[NE, 1], [1, NE]])))
        onef = self.arF[:, 8192:8193]
        mo = P.ve(lambda e: e.memset(onef, 1.0), inc=True, deps=(prev,))
        tl = None
        for ch in range(C):
            dd = (drw, mo, self.bank_wait(3)) if ch == 0 else ()
            tl = P.pe(lambda e, ch=ch: e.matmul(
                self.pb[3][0:1, 0:NE], onef,
                self.rw_lhsT[:, ch * NE:(ch + 1) * NE],
                start=(ch == 0), stop=(ch == C - 1)),
                inc=(ch == C - 1), deps=dd)
        e1 = P.ve(lambda e: e.tensor_copy(self.colw_row[0:1, :],
                                          self.pb[3][0:1, 0:NE]),
                  inc=True, deps=(tl,))
        self.bank_consumed(3, e1)
        t_rt = P.pe(lambda e: e.matmul(
            self.pb[3][:, 0:NE], self.ones_lhsT_f[0:1, :], self.rbias_row[0:1, :],
            start=True, stop=True), inc=True,
            deps=(drb, self.bank_wait(3), self.tok("ve")))
        e2 = P.ve(lambda e: e.tensor_copy(self.rbias_tok[:, :],
                                          self.pb[3][:, 0:NE]),
                  inc=True, deps=(t_rt,))
        self.bank_consumed(3, e2)

        wrow_f = self.arF[0:1, 4096:4096 + 1024]
        whi_f = self.arF[0:1, 4096 + 1024:4096 + 2048]
        prev_dma = drb
        for idx in range(13):
            if idx < 6:
                src, off = self.t_ln1w, idx * D
            elif idx < 12:
                src, off = self.t_ln2w, (idx - 6) * D
            else:
                src, off = self.t_lnfw, 0
            dl = P.dma("sy_q", "st", lambda e, src=src, off=off: e.dma_start(
                wrow_f, bass.AP(src, off, [[D, 1], [1, D]])),
                deps=(prev_dma, self.buf_tok.get("wrowf_free")))
            h1 = P.ve(lambda e: e.tensor_copy(self.rowstage[0:1, 0:1024],
                                              wrow_f),
                      inc=True,
                      deps=(dl,) + tuple(self.buf_tok.get("rowstage", ())))
            h2 = P.ve(lambda e: e.tensor_copy(self.rowstage[0:1, 1024:2048],
                                              wrow_f), inc=True, deps=(dl,))
            h3 = P.ve(lambda e: e.tensor_copy(whi_f, self.rowstage[0:1, 0:1024]),
                      inc=True, deps=(h1,))
            h4 = P.ve(lambda e: e.tensor_sub(self.rowstage[0:1, 2048:3072],
                                             wrow_f, whi_f), inc=True,
                      deps=(h3,))
            self.buf_tok["wrowf_free"] = h4
            ds = P.dma("sc_q", "lt", lambda e, idx=idx: e.dma_start(
                bass.AP(self.t_LT, idx * 1024, [[14336, 3], [1, 1024]]),
                self.rowstage[0:1, 0:3072]), deps=(h2, h4))
            ds2 = P.dma("sc_q", "lt", lambda e, idx=idx: e.dma_start(
                bass.AP(self.t_LT, 32 * 14336 + idx * 1024,
                        [[14336, 3], [1, 1024]]),
                self.rowstage[0:1, 0:3072]))
            self.buf_tok["rowstage"] = (ds, ds2)
            prev_dma = dl
        mz1 = P.ve(lambda e: e.memset(self.rowstage[0:1, 0:256], 1.0),
                   inc=True, deps=tuple(self.buf_tok["rowstage"]))
        mz2 = P.ve(lambda e: e.memset(self.rowstage[0:1, 256:384], 0.0),
                   inc=True)
        dmac = P.dma("sc_q", "lt", lambda e: e.dma_start(
            bass.AP(self.t_LT, 13 * 1024, [[14336, 3], [1, 128]]),
            self.rowstage[0:1, 0:384]), deps=(mz1, mz2))
        dmac2 = P.dma("sc_q", "lt", lambda e: e.dma_start(
            bass.AP(self.t_LT, 32 * 14336 + 13 * 1024, [[14336, 3], [1, 128]]),
            self.rowstage[0:1, 0:384]))
        self.buf_tok["rowstage"] = (dmac, dmac2)
        dls = []
        for pb_, do in ((0, 0), (32, 32 * 14336)):
            dls.append(P.dma("sy_q", "st", lambda e, pb_=pb_, do=do: e.dma_start(
                self.lnLTs[pb_:pb_ + 3, 0:1024],
                bass.AP(self.t_LT, do + 12 * 1024, [[14336, 3], [1, 1024]])),
                deps=(prev_dma, dmac2)))
            dls.append(P.dma("sy_q", "st", lambda e, pb_=pb_, do=do: e.dma_start(
                self.lnLTs[pb_:pb_ + 3, 1024:1152],
                bass.AP(self.t_LT, do + 13 * 1024, [[14336, 3], [1, 128]])),
                deps=(dmac2,)))
        self.buf_tok["setup"] = (dE2, e1, e2, self.tok("ve")) + tuple(dls)

    def emit_ffn(self, call, src_stream=None):
        P = self
        A_ACC, A_STG = 0, 8192
        B_XN, B_U, B_W = 0, 8192, 24576
        wslot_load = [None, None, None]
        wslot_free = [None, None, None]

        def acc(c, g=None):
            if g is None:
                return self.aF(A_ACC + c * 1024, 1024)
            return self.aF(A_ACC + c * 1024 + g * NT, NT)

        def xn(c, g):
            return self.aB(B_XN + c * 1024 + g * NT, NT)

        def uu(ci, g):
            return self.aB(B_U + ci * 1024 + g * NT, NT)

        af = self.buf_tok.get("arB_free")
        if af is not None:
            wslot_free[:] = [af, af, af]
            for ci in range(CI):
                for g_ in range(2):
                    self.buf_tok[("ufree", ci, g_)] = af
            self.buf_tok["apply_arB"] = af
        mats = [("g", self.t_sg, 0), ("u", self.t_su, 0), ("d", self.t_sd, 0)]
        for e in range(NE):
            mats += [("g", self.t_rg, e * D * DI), ("u", self.t_ru, e * D * DI),
                     ("d", self.t_rd, e * DI * D)]

        slot_i = [0]

        def load_half(mi, half):
            s = slot_i[0]
            slot_i[0] = (s + 1) % 3
            tok = self.wb_load(f"ws{s}", mi * 2 + half, B_W + s * 8192,
                               deps=(wslot_free[s],) +
                                    tuple(self.buf_tok.get("setup", ())))
            wslot_load[s] = tok
            return s

        def wtile(s, kc, mc):
            return self.aB(B_W + s * 8192 + kc * 1024 + mc * 128, 128)

        out_dma_toks = []
        for p in range(NPASS):
            tok0 = p * 1024
            in_toks = []
            if call == 0:
                stg = self.aF(A_STG, 1024)
                for tc in range(8):
                    dl = P.dma("sy_q", "stg", lambda e, tc=tc, tok0=tok0:
                               e.dma_start(
                        stg, bass.AP(self.t_x, (tok0 + tc * 128) * D,
                                     [[D, 128], [1, D]])),
                        deps=(self.buf_tok.get("ffn_stg"),) +
                             tuple(self.buf_tok.get("setup", ())))
                    last_tp = None
                    for c in range(C):
                        bk = 4 + (c % 2)
                        tp = self.transpose_into(
                            bk, stg[:, c * 128:(c + 1) * 128], 128,
                            dep=(dl, self.buf_tok.get(("accfree", c))))
                        ev = P.ve(lambda e, b=bk, c=c, tc=tc: e.tensor_copy(
                            acc(c)[:, tc * 128:(tc + 1) * 128],
                            self.pb[b][:, 0:128]), inc=True, deps=(tp,))
                        self.bank_consumed(bk, ev)
                        in_toks.append(ev)
                        last_tp = tp
                    self.buf_tok["ffn_stg"] = last_tp
                in_all = (in_toks[-1],)
                in_c = {c: in_all for c in range(C)}
            else:
                in_c = {}
                for c in range(C):
                    dl = P.dma("sy_q", f"acc{c}", lambda e, c=c, tok0=tok0:
                               e.dma_start(
                        acc(c), bass.AP(src_stream, c * 128 * T + tok0,
                                        [[T, 128], [1, 1024]])),
                        deps=tuple(self.buf_tok.get("setup", ())) +
                             (self.buf_tok.get(("accfree", c)),))
                    in_c[c] = (dl,)

            row_toks = {}
            if True:
                for g in range(2):
                    for c in range(C):
                        s1, s2 = self.stats_chunk(
                            acc(c, g), in_c[c], c == 0, c == C - 1, 6, 7, "f")
                    dmas, ta, tb5 = self.ln_rows(
                        6, 7, (s1, s2), NT, self.RS[0], tok0 + g * NT,
                        keep_ab_at=g * NT)
                    row_toks[g] = dmas
                    row_toks[(g, "B")] = tb5
                    for s in range(4):
                        tp = self.transpose_into(
                            5, self.r_a[0:1,
                                        g * NT + s * 128:g * NT + (s + 1) * 128],
                            128, dep=(ta,))
                        ev = P.ve(lambda e, gs=g * 4 + s: e.tensor_copy(
                            self.a_colT[:, gs:gs + 1], self.pb[5][0:128, 0:1]),
                            inc=True, deps=(tp,))
                        self.bank_consumed(5, ev)
                        row_toks[(g, "acol", s)] = ev

            apply_all = []
            for g in range(2):
                rs_use, col_use = self.RS[0], tok0 + g * NT
                rdep = row_toks[g] + tuple(self.buf_tok.get("setup", ())) + \
                    (self.buf_tok.get("apply_arB"),)
                for c in range(C):
                    vf, vb = self.apply_ln(
                        self.lnLTs, 1024, 128, rs_use, col_use, NT, acc(c, g),
                        None, xn(c, g), 0.0, in_c[c], rs_dep=rdep)
                    apply_all.append(vb)
            apply_done = (apply_all[-1],)

            wrowF_tok = {}
            for g in range(2):
                dep0 = (self.bank_wait(5), self.tok("ve")) + in_c[0]
                for c in range(C):
                    P.pe(lambda e, c=c, g=g: e.matmul(
                        self.pb[5][0:NE, 0:NT],
                        self.rw_lhsT[:, c * NE:(c + 1) * NE],
                        acc(c, g), start=(c == 0), stop=False),
                        deps=dep0 if c == 0 else in_c[c])
                tcor = P.pe(lambda e, g=g: e.matmul(
                    self.pb[5][0:NE, 0:NT], self.colw_row[0:1, :],
                    self.r_b[0:1, g * NT:(g + 1) * NT],
                    start=False, stop=True), inc=True,
                    deps=(row_toks[(g, "B")],))
                cl = P.ve(lambda e: e.tensor_copy(self.lg_sb[0:NE, 0:NT],
                                                  self.pb[5][0:NE, 0:NT]),
                          inc=True, deps=(tcor,))
                self.bank_consumed(5, cl)
                wr_toks = []
                for s in range(4):
                    sl = slice(s * NE, (s + 1) * NE)
                    tp = self.transpose_into(
                        4, self.lg_sb[0:NE, s * 128:(s + 1) * 128], 128,
                        dep=(cl,))
                    st = P.ve(lambda e, sl=sl, gs=g * 4 + s: e.scalar_tensor_tensor(
                        self.lgt[:, sl], self.pb[4][:, 0:NE],
                        self.a_colT[:, gs:gs + 1],
                        self.rbias_tok[:, :], ALU.mult, ALU.add),
                        inc=True, deps=(tp, row_toks[(g, "acol", s)]))
                    self.bank_consumed(4, st)
                    pa = P.ae(lambda e, sl=sl: e.activation(
                        self.pr_t[:, sl], self.lgt[:, sl], AF.Sigmoid,
                        bias=np.float32(0.0), scale=np.float32(1.0)), inc=True, deps=(st,))
                    v1 = P.ve(lambda e, sl=sl, s=s: e.reduce_max(
                        self.m_col[:, s:s + 1], self.lgt[:, sl], AXL.X),
                        inc=True, deps=(st,))
                    v2 = P.ve(lambda e, sl=sl, s=s: e.tensor_scalar(
                        self.msk[:, sl], self.lgt[:, sl],
                        self.m_col[:, s:s + 1], None, ALU.is_ge),
                        inc=True, deps=(v1,))
                    v3 = P.ve(lambda e, sl=sl: e.scalar_tensor_tensor(
                        self.w_tok[:, sl], self.msk[:, sl], -BIG,
                        self.lgt[:, sl], ALU.mult, ALU.add),
                        inc=True, deps=(v2,))
                    v4 = P.ve(lambda e, sl=sl, s=s: e.reduce_max(
                        self.m_col[:, 4 + s:5 + s], self.w_tok[:, sl], AXL.X),
                        inc=True, deps=(v3,))
                    v5 = P.ve(lambda e, sl=sl, s=s: e.tensor_scalar(
                        self.msk[:, sl], self.lgt[:, sl],
                        self.m_col[:, 4 + s:5 + s], None, ALU.is_ge),
                        inc=True, deps=(v4,))
                    v6 = P.ve(lambda e, sl=sl: e.tensor_mul(
                        self.w_tok[:, sl], self.pr_t[:, sl], self.msk[:, sl]),
                        inc=True, deps=(v5, pa))
                    v7 = P.ve(lambda e, sl=sl, s=s: e.reduce_sum(
                        self.m_col[:, 8 + s:9 + s], self.w_tok[:, sl], AXL.X),
                        inc=True, deps=(v6,))
                    a1 = P.ve(lambda e, s=s: e.reciprocal(
                        self.m_col[:, 8 + s:9 + s],
                        self.m_col[:, 8 + s:9 + s]), inc=True, deps=(v7,))
                    v8 = P.ve(lambda e, sl=sl, s=s: e.tensor_scalar(
                        self.w_tok[:, sl], self.w_tok[:, sl],
                        self.m_col[:, 8 + s:9 + s], 0.5, ALU.mult, ALU.mult),
                        inc=True, deps=(a1, v6))
                    tp2 = self.transpose_into(4, self.w_tok[:, sl], NE,
                                              dep=(v8,))
                    ev = P.ve(lambda e, s=s, g=g: e.tensor_copy(
                        self.wrow[0:NE,
                                  g * NT + s * 128:g * NT + (s + 1) * 128],
                        self.pb[4][0:NE, 0:128]), inc=True,
                        deps=(tp2, self.buf_tok.get("wrow_free")))
                    self.bank_consumed(4, ev)
                    wr_toks.append(ev)
                wrowF_tok[g] = tuple(wr_toks)
            self.buf_tok["ra_free"] = self.tok("pe")
            for mi, (kind, tt, eoff) in enumerate(mats):
                ei = mi // 3
                s0 = load_half(mi, 0)
                s1_ = load_half(mi, 1)
                if kind in ("g", "u"):
                    for hf, s in ((0, s0), (1, s1_)):
                        last_mm = None
                        for g in range(2):
                            for mc in range(8):
                                ci = hf * 8 + mc
                                bk = mc % 2
                                dd = (self.bank_wait(bk), wslot_load[s]) + \
                                     apply_done
                                for kc in range(C):
                                    tmm = P.pe(lambda e, s=s, kc=kc, mc=mc,
                                               g=g, bk=bk: e.matmul(
                                        self.pb[bk][:, 0:NT],
                                        wtile(s, kc, mc), xn(kc, g),
                                        start=(kc == 0), stop=(kc == C - 1)),
                                        inc=(kc == C - 1),
                                        deps=dd if kc == 0 else ())
                                last_mm = tmm
                                if kind == "g":
                                    ev = P.ae(lambda e, bk=bk, ci=ci, g=g:
                                              e.activation(
                                        uu(ci, g), self.pb[bk][:, 0:NT],
                                        AF.Silu, bias=0.0, scale=1.0),
                                        inc=True,
                                        deps=(tmm,
                                              self.buf_tok.get(("ufree", ci, g))))
                                else:
                                    ev = P.ve(lambda e, bk=bk, ci=ci, g=g:
                                              e.tensor_mul(
                                        uu(ci, g), self.pb[bk][:, 0:NT],
                                        uu(ci, g)), inc=True, deps=(tmm,))
                                self.bank_consumed(bk, ev)
                                self.buf_tok[("uready", ci, g)] = ev
                        wslot_free[s] = last_mm
                else:
                    last_mm = None
                    for g in range(2):
                        if ei > 0:
                            dfw = P.dma("sc_q", "wrf", lambda e, g=g, ei=ei:
                                        e.dma_start(
                                self.wrowF[0:1, 0:NT],
                                bass.AP(self.wrow, (ei - 1) * 1024 + g * NT,
                                        [[1024, 1], [1, NT]])),
                                deps=(self.buf_tok.get("wbF"),) +
                                     wrowF_tok[g])
                            twb = P.pe(lambda e: e.matmul(
                                self.pb[4][:, 0:NT], self.ones_lhsT[0:1, :],
                                self.wrowF[0:1, 0:NT],
                                start=True, stop=True), inc=True,
                                deps=(self.bank_wait(4), dfw))
                            self.buf_tok["wbF"] = twb
                            cwb = P.ve(lambda e: e.tensor_copy(
                                self.wbs[:, 0:NT], self.pb[4][:, 0:NT]),
                                inc=True, deps=(twb,))
                            self.bank_consumed(4, cwb)
                        for mc in range(C):
                            bk = 2 + (mc % 2)
                            uready = tuple(self.buf_tok.get(("uready", kk, g))
                                           for kk in range(CI))
                            dd = (self.bank_wait(bk), wslot_load[s0],
                                  wslot_load[s1_]) + uready
                            for kk in range(CI):
                                hh, kcl = divmod(kk, 8)
                                s_ = s0 if hh == 0 else s1_
                                tmm = P.pe(lambda e, s_=s_, kcl=kcl, mc=mc,
                                           g=g, bk=bk, kk=kk: e.matmul(
                                    self.pb[bk][:, 0:NT],
                                    wtile(s_, kcl, mc), uu(kk, g),
                                    start=(kk == 0), stop=(kk == CI - 1)),
                                    inc=(kk == CI - 1),
                                    deps=dd if kk == 0 else ())
                            last_mm = tmm
                            if ei == 0:
                                ev = P.ve(lambda e, bk=bk, mc=mc, g=g:
                                          e.scalar_tensor_tensor(
                                    acc(mc, g), self.pb[bk][:, 0:NT], 0.5,
                                    acc(mc, g), ALU.mult, ALU.add),
                                    inc=True, deps=(tmm,))
                            else:
                                ev0 = P.ve(lambda e, bk=bk: e.tensor_mul(
                                    self.tA[:, 0:NT], self.pb[bk][:, 0:NT],
                                    self.wbs[:, 0:NT]), inc=True,
                                    deps=(tmm, cwb,
                                          self.buf_tok.get("ffn_tA")))
                                ev = P.ve(lambda e, mc=mc, g=g: e.tensor_add(
                                    acc(mc, g), self.tA[:, 0:NT], acc(mc, g)),
                                    inc=True, deps=(ev0,))
                                self.buf_tok["ffn_tA"] = ev
                            self.bank_consumed(bk, ev)
                    for g in range(2):
                        for kk in range(CI):
                            self.buf_tok[("ufree", kk, g)] = last_mm
                    wslot_free[s0] = last_mm
                    wslot_free[s1_] = last_mm

            last_acc = self.tok("ve")
            if call == 0:
                for g in range(2):
                    for c in range(C):
                        s1, s2 = self.stats_chunk(
                            acc(c, g), (last_acc,), c == 0, c == C - 1,
                            6, 7, "f")
                    self.ln_rows(6, 7, (s1, s2), NT, self.RS[1],
                                 tok0 + g * NT)
                dto = P.dma("sc_q", "x1o", lambda e, tok0=tok0: e.dma_start(
                    bass.AP(self.t_X1, tok0,
                            [[T, 128], [128 * T, C], [1, 1024]]),
                    self.aF(A_ACC, 8192)), deps=(last_acc,))
                out_dma_toks.append(dto)
                for c in range(C):
                    self.buf_tok[("accfree", c)] = dto
                if p == NPASS - 1:
                    guard = (self.tok("ve"), self.tok("pe"), dto)
                    self.buf_tok[("inF", 0)] = guard
                    self.buf_tok[("inF", 1)] = guard
            else:
                stg = self.aF(A_STG, 1024)
                for tc in range(8):
                    evs = []
                    for c in range(C):
                        bk = 4 + (c % 2)
                        tp = self.transpose_into(
                            bk, acc(c)[:, tc * 128:(tc + 1) * 128], 128,
                            dep=(last_acc, self.buf_tok.get("ffn_stg2")))
                        ev = P.ve(lambda e, b=bk, c=c: e.tensor_copy(
                            stg[:, c * 128:(c + 1) * 128],
                            self.pb[b][:, 0:128]), inc=True, deps=(tp,))
                        self.bank_consumed(bk, ev)
                        evs.append(ev)
                    dto = P.dma("sc_q", "outo", lambda e, tc=tc, tok0=tok0:
                                e.dma_start(
                        bass.AP(self.t_out, (tok0 + tc * 128) * D,
                                [[D, 128], [1, D]]), stg), deps=tuple(evs))
                    self.buf_tok["ffn_stg2"] = dto
                    out_dma_toks.append(dto)
                for c in range(C):
                    self.buf_tok[("accfree", c)] = out_dma_toks[-1]
        self.buf_tok["arB_free"] = (self.tok("pe"), self.tok("ve"),
                                    self.tok("ae"))
        return out_dma_toks


    def emit_attn_layer(self, li, ein, mid, xn1buf, zout, entry):
        P = self
        VTOK = 37888
        XBF, KTOK = 32768, 36864
        QBF, OBF = 36864, 40960
        XN1, USTG = 32768, 36864

        def inF(t):
            return (t % 2) * 4096

        def arb(off, n):
            return self.aB(off, n)

        dbk = P.dma("gp_q", "bk", lambda e, li=li: e.dma_start(
            self.bk_row[0:1, :], bass.AP(self.t_bk, li * D, [[D, 1], [1, D]])),
            deps=(self.buf_tok.get("bk_consumed"),))
        dbv = P.dma("gp_q", "bv", lambda e, li=li: e.dma_start(
            self.bv_row[0:1, :], bass.AP(self.t_bv, li * D, [[D, 1], [1, D]])),
            deps=(self.buf_tok.get("bv_consumed"),))

        arB_free = (self.buf_tok.get("arB_free"),)
        u0 = 48 + li * 12
        dwk = self.wb_load("aw0", u0 + 0, 0, arB_free)
        dwv = self.wb_load("aw1", u0 + 1, 8192, arB_free)
        entry_lhsT_t, entry_c0, entry_bias = entry
        entry_step = 0 if entry_bias is None else 128
        rowdep = (self.dma_last.get("d_rs_a"), self.dma_last.get("d_rs_b"),
                  self.buf_tok.get("lnLT_load"))

        mvo = P.ve(lambda e: e.memset(
            bass.AP(self.arB, VTOK + 128, [[49152, 128], [130, 8], [1, 1]]),
            1.0), inc=True, deps=arB_free + (self.buf_tok.get("ktokvtok_free"),))
        self.buf_tok["vtok_ones"] = mvo
        kv_first = [True]
        for t in range(NTILES):
            tok0 = t * NT
            base = inF(t)
            din = P.dma("sy_q", f"ain{t % 2}", lambda e, tok0=tok0, ein=ein,
                        base=base: e.dma_start(
                self.aF(base, 4096),
                bass.AP(ein, tok0, [[T, 128], [128 * T, C], [1, 512]])),
                deps=(self.buf_tok.get(("inF", t % 2)),))
            xin = lambda c, base=base: self.aF(base + c * 512, 512)
            ap_toks = []
            for c in range(C):
                eb = 0.0 if entry_bias is None else \
                    self.lnb_col[:, entry_bias + c:entry_bias + c + 1]
                vf, vb = self.apply_ln(
                    entry_lhsT_t, entry_c0 + c * entry_step, 128, self.RS[1],
                    tok0, NT, xin(c), None, arb(XBF + c * 512, 512), eb,
                    (din,), rs_dep=rowdep + (self.buf_tok.get("xinbf_free"),))
                ap_toks.append(vb)
            self.buf_tok[("inF", t % 2)] = ap_toks[-1]
            for q in range(4):
                kev = {}
                for isv, (wsl, wld, brow, banks) in enumerate(
                        ((0, dwk, self.bk_row, (0, 1)),
                         (1, dwv, self.bv_row, (2, 6)))):
                    for nh in range(2):
                        bk = banks[nh]
                        dd = (self.bank_wait(bk), wld, ap_toks[-1],
                              dbk if isv == 0 else dbv)
                        for kc in range(C):
                            tmm = P.pe(lambda e, wsl=wsl, kc=kc, nh=nh, q=q,
                                       base=base, bk=bk: e.matmul(
                                self.pb[bk][:, 0:NT],
                                self.aB(base * 0 + XBF + kc * 512 + q * 128, 128),
                                self.aB(wsl * 8192 + kc * 1024 + nh * 512, 512),
                                start=(kc == 0), stop=False),
                                deps=dd if kc == 0 else ())
                        tmm = P.pe(lambda e, brow=brow, nh=nh, bk=bk: e.matmul(
                            self.pb[bk][:, 0:NT], self.ones_lhsT[0:1, :],
                            brow[0:1, nh * 512:(nh + 1) * 512],
                            start=False, stop=True), inc=True)
                        if isv == 0:
                            ep = self.buf_tok.get("elu_prev", ())
                            a1 = P.ae(lambda e, bk=bk: e.activation(
                                self.tB[:, 0:NT], self.pb[bk][:, 0:NT],
                                AF.Relu, bias=np.float32(0.0),
                                scale=np.float32(1.0)), inc=True,
                                deps=(tmm,) + tuple(ep))
                            v1 = P.ve(lambda e, bk=bk: e.tensor_scalar(
                                self.tA[:, 0:NT], self.pb[bk][:, 0:NT],
                                0.0, None, ALU.min), inc=True,
                                deps=(tmm, a1) + tuple(ep))
                            a2 = P.ae(lambda e: e.activation(
                                self.tC[:, 0:NT], self.tA[:, 0:NT], AF.Exp,
                                bias=np.float32(0.0), scale=np.float32(1.0)),
                                inc=True, deps=(v1,))
                            v2 = P.ve(lambda e, nh=nh: e.tensor_add(
                                self.aB(KTOK + nh * 512, 512),
                                self.tB[:, 0:NT], self.tC[:, 0:NT]),
                                inc=True,
                                deps=(a2, a1,
                                      self.buf_tok.get("ktokvtok_free")))
                            self.bank_consumed(bk, v1)
                            self.buf_tok["elu_prev"] = (a2, v2)
                            kev[nh] = v2
                        else:
                            vv = P.ve(lambda e, nh=nh, bk=bk: e.tensor_copy(
                                bass.AP(self.arB, VTOK + nh * 4 * 130,
                                        [[49152, 128], [130, 4], [1, 128]]),
                                self.pb[bk][:, 0:NT]), inc=True,
                                deps=(tmm,
                                      self.buf_tok.get("ktokvtok_free")))
                            self.bank_consumed(bk, vv)
                            kev[2 + nh] = vv
                kvt = None
                for pr in range(NPAIR):
                    b = 3 + pr // 3
                    off = (pr % 3) * 130
                    dd = [kev[pr // 4], kev[2 + pr // 4]]
                    if kv_first[0]:
                        dd.append(self.bank_wait(b))
                        dd.append(self.buf_tok.get("vtok_ones"))
                    last = (t == NTILES - 1 and q == 3)
                    st_ = kv_first[0] and (pr % 3 == 0)
                    sp_ = last and (pr % 3 == 2 or pr == 7)
                    tkv = P.pe(lambda e, pr=pr, b=b, off=off, st_=st_, sp_=sp_:
                               e.matmul(
                        self.pb[b][:, off:off + 129],
                        self.aB(KTOK + pr * 128, 128),
                        bass.AP(self.arB, VTOK + pr * 130,
                                [[49152, 128], [1, 129]]),
                        start=st_, stop=sp_),
                        inc=(last or pr == 7), deps=tuple(dd))
                    if last:
                        self.buf_tok[("kv", pr)] = tkv
                    kvt = tkv
                self.buf_tok["ktokvtok_free"] = kvt
                if kv_first[0]:
                    kv_first[0] = False
            self.buf_tok["xinbf_free"] = self.tok("pe")

        self.buf_tok["aw0_free"] = self.tok("pe")
        self.buf_tok["aw1_free"] = self.tok("pe")
        zm = P.ve(lambda e: e.memset(self.zsb[:, :], 0.0), inc=True,
                  deps=(self.buf_tok.get("zsb_consumed"),))
        bd_toks = []
        for pr in range(NPAIR):
            b = 3 + pr // 3
            off = (pr % 3) * 130
            kvd = (self.buf_tok[("kv", pr)], self.buf_tok.get("BD_consumed"))
            m1 = P.ve(lambda e, pr=pr: e.memset(
                self.BD[0:64, pr * 128 + 64:pr * 128 + 128], 0.0),
                inc=True, deps=kvd)
            m2 = P.ve(lambda e, pr=pr: e.memset(
                self.BD[64:128, pr * 128:pr * 128 + 64], 0.0), inc=True)
            c1 = P.ve(lambda e, pr=pr, b=b, off=off: e.tensor_copy(
                self.BD[0:64, pr * 128:pr * 128 + 64],
                self.pb[b][0:64, off:off + 64]), inc=True)
            c2 = P.ve(lambda e, pr=pr, b=b, off=off: e.tensor_copy(
                self.BD[64:128, pr * 128 + 64:pr * 128 + 128],
                self.pb[b][64:128, off + 64:off + 128]), inc=True)
            z1 = P.ve(lambda e, pr=pr, b=b, off=off: e.tensor_copy(
                self.zsb[0:64, pr * 2:pr * 2 + 1],
                self.pb[b][0:64, off + 128:off + 129]), inc=True, deps=(zm,))
            z2 = P.ve(lambda e, pr=pr, b=b, off=off: e.tensor_copy(
                self.zsb[64:128, pr * 2 + 1:pr * 2 + 2],
                self.pb[b][64:128, off + 128:off + 129]), inc=True)
            bd_toks += [m1, m2, c1, c2, z1, z2]
            self.bank_consumed(b, z2)
        bd_done = bd_toks[-1]

        dwq = self.wb_load("aw2", u0 + 2, 2 * 8192,
                           (self.buf_tok.get("aw2_free"),))
        dwo = self.wb_load("aw3", u0 + 3, 3 * 8192,
                           (self.buf_tok.get("aw3_free"),))
        dW1pre = [self.wb_load("aw0", u0 + 4, 0, (self.buf_tok.get("aw0_free"),)),
                  self.wb_load("aw1", u0 + 5, 8192,
                               (self.buf_tok.get("aw1_free"),))]
        for t in range(NTILES):
            tok0 = t * NT
            base = inF(t)
            din = P.dma("sy_q", f"ain{t % 2}", lambda e, tok0=tok0, ein=ein,
                        base=base: e.dma_start(
                self.aF(base, 4096),
                bass.AP(ein, tok0, [[T, 128], [128 * T, C], [1, 512]])),
                deps=(self.buf_tok.get(("inF", t % 2)),))
            xin = lambda c, base=base: self.aF(base + c * 512, 512)
            ap_toks = []
            for c in range(C):
                eb = 0.0 if entry_bias is None else \
                    self.lnb_col[:, entry_bias + c:entry_bias + c + 1]
                vf, vb = self.apply_ln(
                    entry_lhsT_t, entry_c0 + c * entry_step, 128, self.RS[1],
                    tok0, NT, xin(c), xin(c), arb(XBF + c * 512, 512), eb,
                    (din,), rs_dep=rowdep + (self.buf_tok.get("xinbf_free"),),
                    banks=(4, 5) if c % 2 else (6, 7))
                ap_toks.append(vb)
            qev = []
            for mc in range(C):
                bk = mc % 2
                dd = (self.bank_wait(bk), dwq, ap_toks[-1])
                for kc in range(C):
                    tmm = P.pe(lambda e, kc=kc, mc=mc, bk=bk: e.matmul(
                        self.pb[bk][:, 0:NT],
                        self.aB(2 * 8192 + kc * 1024 + mc * 128, 128),
                        self.aB(XBF + kc * 512, 512),
                        start=(kc == 0), stop=(kc == C - 1)),
                        inc=(kc == C - 1), deps=dd if kc == 0 else ())
                bq = self.bq_col[:, li * C + mc:li * C + mc + 1]
                ep = self.buf_tok.get("elu_prev", ())
                a1 = P.ae(lambda e, bk=bk, bq=bq: e.activation(
                    self.tB[:, 0:NT], self.pb[bk][:, 0:NT], AF.Relu,
                    bias=bq, scale=np.float32(1.0)), inc=True,
                    deps=(tmm,) + tuple(ep))
                v1 = P.ve(lambda e, bk=bk, bq=bq: e.tensor_scalar(
                    self.tA[:, 0:NT], self.pb[bk][:, 0:NT], bq, 0.0,
                    ALU.add, ALU.min), inc=True, deps=(tmm, a1) + tuple(ep))
                a2 = P.ae(lambda e: e.activation(
                    self.tC[:, 0:NT], self.tA[:, 0:NT], AF.Exp,
                    bias=np.float32(0.0), scale=np.float32(1.0)),
                    inc=True, deps=(v1,))
                v2 = P.ve(lambda e, mc=mc: e.tensor_add(
                    self.aB(QBF + mc * 512, 512), self.tB[:, 0:NT],
                    self.tC[:, 0:NT]), inc=True, deps=(a2, a1))
                self.bank_consumed(bk, v1)
                self.buf_tok["elu_prev"] = (a2, v2)
                qev.append(v2)
            z_tok, vr_tok, voc_tok, oev = {}, {}, {}, {}
            zr_slot = (self.zrec[0:2, 0:NT], self.zrec2[0:2, 0:NT],
                       self.wrow[0:2, 0:NT], self.wrow[0:2, NT:2 * NT])

            def emit_zb(pr):
                bz = 4 + (pr % 4)
                zr = zr_slot[pr % 4]
                tzb = P.pe(lambda e, zr=zr, bz=bz: e.matmul(
                    self.pb[bz][:, 0:NT], self.E2[0:2, :], zr,
                    start=True, stop=True), inc=True,
                    deps=(self.bank_wait(bz), vr_tok[pr]))
                self.buf_tok[("b1r", pr % 4)] = tzb
                zbs_ = self.zbs if pr % 2 == 0 else self.wbs
                vzb = P.ve(lambda e, bz=bz, zbs_=zbs_: e.tensor_copy(
                    zbs_[:, 0:NT], self.pb[bz][:, 0:NT]), inc=True,
                    deps=(tzb,))
                self.bank_consumed(bz, vzb)
                vo = P.ve(lambda e, pr=pr, zbs_=zbs_: e.tensor_mul(
                    self.aB(OBF + pr * 512, 512), self.aB(OBF + pr * 512, 512),
                    zbs_[:, 0:NT]), inc=True, deps=(vzb, voc_tok[pr]))
                oev[pr] = vo

            for pr in range(NPAIR):
                bo_ = 2 + (pr % 2)
                bz = 4 + (pr % 4)
                to = P.pe(lambda e, pr=pr, bo_=bo_: e.matmul(
                    self.pb[bo_][:, 0:NT], self.BD[:, pr * 128:(pr + 1) * 128],
                    self.aB(QBF + pr * 512, 512), start=True, stop=True),
                    inc=True, deps=(self.bank_wait(bo_), qev[pr], bd_done))
                voc = P.ve(lambda e, pr=pr, bo_=bo_: e.tensor_copy(
                    self.aB(OBF + pr * 512, 512), self.pb[bo_][:, 0:NT]),
                    inc=True, deps=(to,))
                self.bank_consumed(bo_, voc)
                voc_tok[pr] = voc
                z_tok[pr] = P.pe(lambda e, pr=pr, bz=bz: e.matmul(
                    self.pb[bz][0:2, 0:NT], self.zsb[:, pr * 2:pr * 2 + 2],
                    self.aB(QBF + pr * 512, 512), start=True, stop=True),
                    inc=True, deps=(self.bank_wait(bz), bd_done))
                tb2 = self.tB[0:2, 0:NT] if pr % 2 == 0 else self.tB[32:34, 0:NT]
                ae1 = P.ae(lambda e, bz=bz, tb2=tb2: e.activation(
                    tb2, self.pb[bz][0:2, 0:NT], AF.Identity,
                    bias=np.float32(EPS_ATTN), scale=np.float32(1.0)),
                    inc=True, deps=(z_tok[pr], qev[-1],
                                    self.buf_tok.get(("b1z", pr % 2))))
                self.bank_consumed(bz, ae1)
                if pr >= 4:
                    emit_zb(pr - 4)
                def _vrf2(e, tb2=tb2, zr=zr_slot[pr % 4]):
                    with self.nc.allow_low_precision(
                            reason="z normalizer feeds a bf16 matmul"):
                        return e.reciprocal(zr, tb2)
                vr_tok[pr] = P.ve(_vrf2, inc=True,
                                  deps=(ae1, self.buf_tok.get(("b1r", pr % 4)),
                                        self.buf_tok.get("wrow_free")))
                self.buf_tok[("b1z", pr % 2)] = vr_tok[pr]
            for pr in range(4, NPAIR):
                emit_zb(pr)
            oev = [oev[pr] for pr in range(NPAIR)]
            self.buf_tok["elu_prev"] = (vr_tok[6], vr_tok[7])
            st_toks = []
            for mc in range(C):
                bk = mc % 2
                dd = (self.bank_wait(bk), dwo, oev[-1])
                for kc in range(C):
                    tmm = P.pe(lambda e, kc=kc, mc=mc, bk=bk: e.matmul(
                        self.pb[bk][:, 0:NT],
                        self.aB(3 * 8192 + kc * 1024 + mc * 128, 128),
                        self.aB(OBF + kc * 512, 512),
                        start=(kc == 0), stop=(kc == C - 1)),
                        inc=(kc == C - 1), deps=dd if kc == 0 else ())
                sv = P.ve(lambda e, mc=mc, bk=bk, base=base: e.scalar_tensor_tensor(
                    self.aF(base + mc * 512, 512), self.pb[bk][:, 0:NT],
                    self.bo_col[:, li * C + mc:li * C + mc + 1],
                    self.aF(base + mc * 512, 512), ALU.add, ALU.add),
                    inc=True, deps=(tmm,))
                self.bank_consumed(bk, sv)
                st_toks.append(sv)
            self.buf_tok["xinbf_free"] = self.tok("pe")
            for c in range(C):
                s1, s2 = self.stats_chunk(
                    xin(c), (st_toks[c],), c == 0, c == C - 1, 2, 3, "a")
            self.ln_rows(2, 3, (s1, s2), NT, self.RS[0], tok0)
            dout = P.dma("sc_q", "b1out", lambda e, tok0=tok0, mid=mid,
                         base=base: e.dma_start(
                bass.AP(mid, tok0, [[T, 128], [128 * T, C], [1, 512]]),
                self.aF(base, 4096)), deps=tuple(st_toks))
            self.buf_tok[("inF", t % 2)] = (dout, self.tok("ve"))
        self.buf_tok["wrow_free"] = self.tok("pe")
        self.buf_tok["BD_consumed"] = self.tok("pe")
        self.buf_tok["zsb_consumed"] = self.tok("pe")
        self.buf_tok["bk_consumed"] = self.tok("pe")
        self.buf_tok["bv_consumed"] = self.tok("pe")

        ltl = []
        pedep = (self.tok("pe"),)
        for (dcol, inst) in ((0, li), (1024, 6 + li)):
            for pb_ in (0, 32):
                ltl.append(P.dma("sy_q", "lnlt", lambda e, dcol=dcol,
                                 inst=inst, pb_=pb_: e.dma_start(
                    self.lnLT[pb_:pb_ + 3, dcol:dcol + 1024],
                    bass.AP(self.t_LT, pb_ * 14336 + inst * 1024,
                            [[14336, 3], [1, 1024]])), deps=pedep))
        dlt = dlt2 = ltl[-1]
        self.buf_tok["lnLT_load"] = tuple(ltl)
        ln1_rowdep = (self.dma_last.get("d_rs_a"), self.dma_last.get("d_rs_b"))
        dW1 = dW1pre + [
            self.wb_load(f"aw{qi}", u0 + 4 + qi, qi * 8192,
                         (self.tok("pe"),)) for qi in (2, 3)]
        for t in range(NTILES):
            tok0 = t * NT
            base = inF(t)
            din = P.dma("sy_q", f"ain{t % 2}", lambda e, tok0=tok0, mid=mid,
                        base=base: e.dma_start(
                self.aF(base, 4096),
                bass.AP(mid, tok0, [[T, 128], [128 * T, C], [1, 512]])),
                deps=(self.buf_tok.get(("inF", t % 2)),))
            xin = lambda c, base=base: self.aF(base + c * 512, 512)
            ap_toks = []
            for c in range(C):
                vf, vb = self.apply_ln(
                    self.lnLT, c * 128, 128, self.RS[0], tok0, NT,
                    xin(c), xin(c), arb(XN1 + c * 512, 512),
                    self.lnb_col[:, li * C + c:li * C + c + 1],
                    (din,), rs_dep=ln1_rowdep + (dlt, dlt2),
                    banks=(4, 5) if c % 2 else (6, 7))
                ap_toks.append(vb)
            ust = {}
            for mc in range(CF):
                bk = mc % 4
                qi = mc // 8
                dd = (self.bank_wait(bk), dW1[qi], ap_toks[-1])
                for kc in range(C):
                    tmm = P.pe(lambda e, kc=kc, mc=mc, bk=bk, qi=qi: e.matmul(
                        self.pb[bk][:, 0:NT],
                        self.aB(qi * 8192 + kc * 1024 + (mc % 8) * 128, 128),
                        self.aB(XN1 + kc * 512, 512),
                        start=(kc == 0), stop=(kc == C - 1)),
                        inc=(kc == C - 1), deps=dd if kc == 0 else ())
                g8 = mc // 8
                half = g8 % 2
                ue = P.ae(lambda e, bk=bk, mc=mc, half=half: e.activation(
                    self.aB(USTG + half * 4096 + (mc % 8) * 512, 512),
                    self.pb[bk][:, 0:NT], AF.Relu,
                    bias=self.b1_col[:, li * CF + mc:li * CF + mc + 1],
                    scale=np.float32(1.0)), inc=True,
                    deps=(tmm, self.buf_tok.get(("ustg", half))))
                self.bank_consumed(bk, ue)
                ust.setdefault(g8, []).append(ue)
                if mc % 8 == 7:
                    du = P.dma("sc_q", "uso", lambda e, g8=g8, half=half,
                               tok0=tok0: e.dma_start(
                        bass.AP(self.t_US, g8 * 1024 * T + tok0,
                                [[T, 128], [128 * T, 8], [1, 512]]),
                        self.aB(USTG + half * 4096, 4096)),
                        deps=tuple(ust[g8]))
                    self.buf_tok[("ustg", half)] = du
            dxo = P.dma("sc_q", "b2aout", lambda e, tok0=tok0, base=base,
                        xn1buf=xn1buf: e.dma_start(
                bass.AP(xn1buf, tok0, [[T, 128], [128 * T, C], [1, 512]]),
                self.aF(base, 4096)), deps=tuple(a for a in ap_toks))
            self.buf_tok[("inF", t % 2)] = (dxo,)
        self.buf_tok["aw2_free"] = self.tok("pe")
        self.buf_tok["aw3_free"] = self.tok("pe")

        dW2 = []
        for qi in range(4):
            dW2.append(self.wb_load(f"aw{qi}", u0 + 8 + qi, qi * 8192,
                                    (self.tok("pe"),)))
        for t in range(NTILES):
            tok0 = t * NT
            base = inF(t)
            din = P.dma("sy_q", f"ain{t % 2}", lambda e, tok0=tok0, base=base,
                        xn1buf=xn1buf: e.dma_start(
                self.aF(base, 4096),
                bass.AP(xn1buf, tok0, [[T, 128], [128 * T, C], [1, 512]])),
                deps=(self.buf_tok.get(("inF", t % 2)),))
            xin = lambda c, base=base: self.aF(base + c * 512, 512)
            duh = []
            for kh in range(2):
                duh.append(P.dma("sy_q", f"uh{kh}", lambda e, kh=kh, tok0=tok0:
                                 e.dma_start(
                    self.aB(32768 + kh * 8192, 8192),
                    bass.AP(self.t_US, kh * 2048 * T + tok0,
                            [[T, 128], [128 * T, 16], [1, 512]])),
                    deps=(self.buf_tok.get(("uhf", kh)),
                          self.dma_last.get("d_uso") if t == 0 else None,
                          self.tok("pe") if t == 0 else None)))
            first_mm = {}
            for kh in range(2):
                for j in range(16):
                    kc = kh * 16 + j
                    qi = kc // 8
                    for mc in range(C):
                        dd = [duh[kh], dW2[qi]]
                        if kc == 0:
                            dd.append(self.bank_wait(mc))
                        do_inc = (kc == 31) or (j == 15 and mc == C - 1)
                        tmm = P.pe(lambda e, qi=qi, kc=kc, j=j, kh=kh, mc=mc:
                                   e.matmul(
                            self.pb[mc][:, 0:NT],
                            self.aB(qi * 8192 + (kc % 8) * 1024 + mc * 128, 128),
                            self.aB(32768 + kh * 8192 + j * 512, 512),
                            start=(kc == 0), stop=(kc == 31)),
                            inc=do_inc, deps=tuple(dd) if kc == 0 or mc == 0 else ())
                        if kc == 31:
                            first_mm[mc] = tmm
                self.buf_tok[("uhf", kh)] = tmm
            st_toks = []
            for mc in range(C):
                sv = P.ve(lambda e, mc=mc, base=base: e.scalar_tensor_tensor(
                    self.aF(base + mc * 512, 512), self.pb[mc][:, 0:NT],
                    self.b2_col[:, li * C + mc:li * C + mc + 1],
                    self.aF(base + mc * 512, 512), ALU.add, ALU.add),
                    inc=True, deps=(first_mm[mc], din))
                self.bank_consumed(mc, sv)
                st_toks.append(sv)
            for c in range(C):
                s1, s2 = self.stats_chunk(
                    xin(c), (st_toks[c],), c == 0, c == C - 1, 6, 7, "a")
            self.ln_rows(6, 7, (s1, s2), NT, self.RS[1], tok0)
            dzo = P.dma("sc_q", "b2bout", lambda e, tok0=tok0, base=base,
                        zout=zout: e.dma_start(
                bass.AP(zout, tok0, [[T, 128], [128 * T, C], [1, 512]]),
                self.aF(base, 4096)), deps=tuple(st_toks))
            self.buf_tok[("inF", t % 2)] = (dzo, self.tok("ve"))
        self.buf_tok["arB_free"] = self.tok("pe")
        self.buf_tok["aw2_free2"] = self.tok("pe")
        self.buf_tok["lnLT_consumed"] = self.tok("pe")

    def emit_final(self, zin, x1buf, x2out):
        P = self
        rowdep2 = (self.dma_last.get("d_rs_a"), self.dma_last.get("d_rs_b"))
        for t in range(NTILES):
            tok0 = t * NT
            base = (t % 2) * 4096
            din = P.dma("sy_q", f"ain{t % 2}", lambda e, tok0=tok0, base=base,
                        zin=zin: e.dma_start(
                self.aF(base, 4096),
                bass.AP(zin, tok0, [[T, 128], [128 * T, C], [1, 512]])),
                deps=(self.buf_tok.get(("inF", t % 2)),))
            xin = lambda c, base=base: self.aF(base + c * 512, 512)
            last = None
            for c in range(C):
                vf, vb = self.apply_ln(
                    self.lnLT, 1024 + c * 128, 128, self.RS[1], tok0, NT,
                    xin(c), xin(c), None,
                    self.lnb_col[:, (6 + 5) * C + c:(6 + 5) * C + c + 1],
                    (din,), rs_dep=rowdep2, banks=(4, 5))
                s1, s2 = self.stats_chunk(
                    xin(c), (vf,), c == 0, c == C - 1, 6, 7, "a")
                last = vf
            self.ln_rows(6, 7, (s1, s2), NT, self.RS[0], tok0)
            self.buf_tok[("inF", t % 2)] = (last, self.tok("ve"),
                                            self.tok("pe"))
        lnf_rowdep = (self.dma_last.get("d_rs_a"), self.dma_last.get("d_rs_b"))
        for t in range(NTILES):
            tok0 = t * NT
            base = (t % 2) * 4096
            din = P.dma("sy_q", f"ain{t % 2}", lambda e, tok0=tok0, base=base,
                        zin=zin: e.dma_start(
                self.aF(base, 4096),
                bass.AP(zin, tok0, [[T, 128], [128 * T, C], [1, 512]])),
                deps=(self.buf_tok.get(("inF", t % 2)),))
            xin = lambda c, base=base: self.aF(base + c * 512, 512)
            x1base = ((t + 1) % 2) * 4096
            dx1 = P.dma("sy_q", f"ain{(t + 1) % 2}", lambda e, tok0=tok0,
                        x1base=x1base, x1buf=x1buf: e.dma_start(
                self.aF(x1base, 4096),
                bass.AP(x1buf, tok0, [[T, 128], [128 * T, C], [1, 512]])),
                deps=(self.buf_tok.get(("inF", (t + 1) % 2)),))
            toks = []
            for c in range(C):
                vf, _ = self.apply_ln(
                    self.lnLT, 1024 + c * 128, 128, self.RS[1], tok0, NT,
                    xin(c), xin(c), None,
                    self.lnb_col[:, 11 * C + c:11 * C + c + 1],
                    (din,), rs_dep=rowdep2)
                vf2, _ = self.apply_ln(
                    self.lnLTs, c * 128, 128, self.RS[0], tok0, NT,
                    xin(c), xin(c), None,
                    self.lnb_col[:, 12 * C + c:12 * C + c + 1],
                    (vf,), rs_dep=lnf_rowdep, banks=(4, 5))
                va = P.ve(lambda e, c=c, base=base, x1base=x1base: e.tensor_add(
                    self.aF(base + c * 512, 512),
                    self.aF(x1base + c * 512, 512),
                    self.aF(base + c * 512, 512)), inc=True, deps=(vf2, dx1))
                toks.append(va)
            dxo = P.dma("sc_q", "f2out", lambda e, tok0=tok0, base=base,
                        x2out=x2out: e.dma_start(
                bass.AP(x2out, tok0, [[T, 128], [128 * T, C], [1, 512]]),
                self.aF(base, 4096)), deps=tuple(toks))
            self.buf_tok[("inF", t % 2)] = dxo
            self.buf_tok[("inF", (t + 1) % 2)] = dxo
        for c in range(C):
            self.buf_tok[("accfree", c)] = (self.tok("ve"),
                                            self.dma_last.get("d_f2out"))

    def build(self):
        self.emit_setup()
        self.emit_precast()
        out_toks = self.emit_ffn(call=0)
        if self.debug_stop == "ffn1":
            self.finish(out_toks, debug_src=self.t_X1)
            return
        mac = (self.lnLTs, 1024, None)
        bufs = [self.t_S0, self.t_S1]
        zout = None
        for li in range(NLAYER):
            if li == 0:
                ein, mid, xn1b, zout = self.t_X1, self.t_S1, self.t_S0, self.t_S1
                entry = mac
            else:
                ein = zout
                mid = bufs[0] if ein is bufs[1] else bufs[1]
                xn1b, zout = ein, mid
                entry = (self.lnLT, 1024, (6 + li - 1) * C)
            self.emit_attn_layer(li, ein, mid, xn1b, zout, entry)
            if self.debug_stop == f"layer{li}":
                self.finish(out_toks, debug_src=zout)
                return
        x2buf = bufs[0] if zout is bufs[1] else bufs[1]
        self.emit_final(zout, self.t_X1, x2buf)
        if self.debug_stop == "f2":
            self.finish(out_toks, debug_src=x2buf)
            return
        out_toks = self.emit_ffn(call=1, src_stream=x2buf)
        self.finish(out_toks)

    def finish(self, out_toks, debug_src=None):
        P = self
        if debug_src is not None:
            alldma = tuple(self.dma_last.values())
            d = P.dma("sy_q", "dbg", lambda e: e.dma_start(
                bass.AP(self.t_out, 0, [[T * D, 1], [1, T * D]]),
                bass.AP(debug_src, 0, [[T * D, 1], [1, T * D]])),
                deps=tuple(out_toks) + alldma +
                     (self.tok("ve"), self.tok("pe"), self.tok("ae")))
        for kname, tok in list(self.dma_last.items()):
            self.wait("sync", tok)
        block = self.ctx.enter_context(self.nc.Block())
        for eng in ENGS:
            def _section(e, eng=eng):
                for f in self.ops[eng]:
                    f(e)
            getattr(block, eng)(_section)


def build_nc(debug_stop=None):
    nc = bass.Bass("TRN2", target_bir_lowering=False)
    ctx = contextlib.ExitStack()
    with ctx:
        mk = MK(nc, ctx, debug_stop=debug_stop)
        mk.build()
    return nc


def kernel(**inputs):
    nc = build_nc()
    ident = np.eye(128, dtype=np.float32)
    x = np.ascontiguousarray(inputs["L0_attn_x"])
    in_maps = []
    for c in range(8):
        m = {k: np.ascontiguousarray(v) for k, v in inputs.items()}
        m["L0_attn_x"] = np.ascontiguousarray(x[c])
        m["ident"] = ident
        in_maps.append(m)
    res = run_bass_kernel_spmd(nc, in_maps, core_ids=list(range(8)))
    return np.stack([r["out"] for r in res.results], axis=0)
